# version 17 (final)
# baseline (speedup 1.0000x reference)
import numpy as np
import concourse.bass as bass
import concourse.mybir as mybir
from concourse.bass_utils import run_bass_kernel_spmd

F32 = mybir.dt.float32
F32R = mybir.dt.float32r
BF16 = mybir.dt.bfloat16
FP8 = mybir.dt.float8e4
U8 = mybir.dt.uint8
AF = mybir.ActivationFunctionType
ALU = mybir.AluOpType
AX = mybir.AxisListType.X

D_MODEL = 1024
NH = 16
HD = 128
DCONV = 4
NTIL_X = 17
HID = 4096
TB = 512
CH = 128
NC = 4
NB = 8
T_OWN = 2048
EPS = 1e-5
ENGS = ("tensor", "vector", "scalar", "sync", "gpsimd")


class Em:
    def __init__(self, nc):
        self.nc = nc
        self.items = []
        self.cnt = {}
        self.sems = {}

    def sem(self, name):
        h = self.nc.semaphore(name).__enter__()
        self.sems[name] = h
        self.cnt[name] = 0
        return name

    SELF_CHAIN = {"vector": "s_v", "scalar": "s_sc", "gpsimd": "s_gq"}

    def op(self, eng, fn, wait=(), inc=()):
        ws = []
        sc_sem = self.SELF_CHAIN.get(eng)
        if sc_sem is not None and self.cnt.get(sc_sem, 0) > 0:
            ws.append((sc_sem, self.cnt[sc_sem]))
        for w in wait:
            if w is None:
                continue
            s, v = w
            assert v <= self.cnt[s], f"wait {s}>={v} but only {self.cnt[s]} recorded"
            if v > 0:
                ws.append((s, v))
        for (s, n) in inc:
            self.cnt[s] += n
        self.items.append((eng, fn, ws, list(inc)))
        return None if not inc else (inc[0][0], self.cnt[inc[0][0]])

    def now(self, s):
        return (s, self.cnt[s])

    def emit(self, block):
        def mk(eng_name):
            def emit_fn(e):
                water = {}
                for eng, fn, ws, incs in self.items:
                    if eng != eng_name:
                        continue
                    for (s, v) in ws:
                        if water.get(s, 0) >= v:
                            continue
                        e.wait_ge(self.sems[s], v)
                        water[s] = v
                    ins = fn(e)
                    for (s, n) in incs:
                        ins.then_inc(self.sems[s], n)
            return emit_fn
        for name in ENGS:
            getattr(block, name)(mk(name))


class Pool2D:
    def __init__(self, nc, name, nbytes):
        self.t = nc.sbuf_tensor(name, [128, nbytes], U8).__enter__()
        self.nbytes = nbytes
        self.rects = []

    def rect(self, p0, np_, b0, nb, dtype, shape, tag=""):
        assert b0 + nb <= self.nbytes and p0 + np_ <= 128, (tag, b0 + nb, self.nbytes)
        for (q0, qn, c0, cn, qtag) in self.rects:
            if not (p0 + np_ <= q0 or q0 + qn <= p0 or b0 + nb <= c0 or c0 + cn <= b0):
                raise RuntimeError(f"pool overlap {tag} vs {qtag}")
        self.rects.append((p0, np_, b0, nb, tag))
        ap = self.t[p0:p0 + np_, b0:b0 + nb].bitcast(dtype)
        if len(shape) > 2:
            pat = "p (" + " ".join(chr(97 + i) for i in range(len(shape) - 1)) + ") -> p " + \
                  " ".join(chr(97 + i) for i in range(len(shape) - 1))
            kw = {chr(97 + i): shape[i + 1] for i in range(len(shape) - 2)}
            ap = ap.rearrange(pat, **kw)
        return ap

    def free_rect(self, p0, np_, b0, nb, dtype, shape, tag=""):
        assert b0 + nb <= self.nbytes and p0 + np_ <= 128
        ap = self.t[p0:p0 + np_, b0:b0 + nb].bitcast(dtype)
        if len(shape) > 2:
            pat = "p (" + " ".join(chr(97 + i) for i in range(len(shape) - 1)) + ") -> p " + \
                  " ".join(chr(97 + i) for i in range(len(shape) - 1))
            kw = {chr(97 + i): shape[i + 1] for i in range(len(shape) - 2)}
            ap = ap.rearrange(pat, **kw)
        return ap


def build_program(dbg=False):
    nc = bass.Bass("TRN2", target_bir_lowering=False, debug=False)

    xfm = nc.dram_tensor("xfm", [D_MODEL, 2 * T_OWN], F32, kind="ExternalInput").ap()
    win_t = nc.dram_tensor("win_t", [34, 128, 8, 128], F32, kind="ExternalInput").ap()
    wout_t = nc.dram_tensor("wout_t", [8, 128, 16, 128], F32, kind="ExternalInput").ap()
    wfc = nc.dram_tensor("wfc", [HID, D_MODEL], F32, kind="ExternalInput").ap()
    wpj = nc.dram_tensor("wpj", [D_MODEL, HID], F32, kind="ExternalInput").ap()
    cvw = nc.dram_tensor("cvw", [NTIL_X, 128, DCONV], F32, kind="ExternalInput").ap()
    cvb = nc.dram_tensor("cvb", [128, NTIL_X], F32, kind="ExternalInput").ap()
    alog = nc.dram_tensor("alog", [16, 1], F32, kind="ExternalInput").ap()
    dtbias = nc.dram_tensor("dtbias", [16, 1], F32, kind="ExternalInput").ap()
    dskip = nc.dram_tensor("dskip", [1, 16], F32, kind="ExternalInput").ap()
    ffnu = nc.dram_tensor("ffnu", [16, HID], F32, kind="ExternalInput").ap()
    ffnv = nc.dram_tensor("ffnv", [128, 8, 16], F32, kind="ExternalInput").ap()
    fbt = nc.dram_tensor("fbt", [16, 16], F32, kind="ExternalInput").ap()
    gatei = nc.dram_tensor("gatei", [1, 1], F32, kind="ExternalInput").ap()
    flagi = nc.dram_tensor("flagi", [1, 1], F32, kind="ExternalInput").ap()
    yfm = nc.dram_tensor("yfm", [D_MODEL, T_OWN], F32, kind="ExternalOutput").ap()
    drows = nc.dram_tensor("drows", [4, 16, TB], F32, kind="Internal").ap()
    drows3b = nc.dram_tensor("drows3b", [16, TB], BF16, kind="Internal").ap()
    win_s = nc.dram_tensor("win_s", [34, 128, 8, 128], BF16, kind="Internal").ap()
    wout_s = nc.dram_tensor("wout_s", [8, 128, 16, 128], BF16, kind="Internal").ap()
    x_s = nc.dram_tensor("x_s", [128, 8, 2 * T_OWN], BF16, kind="Internal").ap()
    de = nc.dram_tensor("de", [16, NC], F32, kind="Internal").ap()
    dt_all = nc.dram_tensor("dt_all", [16, 8 * TB], F32, kind="Internal").ap()
    dsc = nc.dram_tensor("dsc", [32, 128], F32, kind="Internal").ap()
    dbg_t = {}
    if dbg:
        for nm, sh in [("d_dt", [16, TB]), ("d_cs", [16, TB]), ("d_xssm", [128, TB]),
                       ("d_scores", [128, NC * 128]), ("d_yg", [128, NH * TB]),
                       ("d_s", [128, 8 * TB]), ("d_x2", [128, 8 * TB]),
                       ("d_r", [64, NH * 128]), ("d_fc8", [128, 1024]),
                       ("d_pj8", [128, 32 * 1024]),
                       ("d_scale", [128, 32])]:
            dbg_t[nm] = nc.dram_tensor(nm, sh, F32, kind="ExternalOutput").ap()

    def sb(name, shape, dt):
        return nc.sbuf_tensor(name, shape, dt).__enter__()

    def psm(name, shape, dt):
        return nc.psum_tensor(name, shape, dt).__enter__()

    fc8 = sb("fc8", [128, 8 * HID], FP8).ap().rearrange("p (k c) -> p k c", k=8)
    pj8 = sb("pj8", [128, 32 * D_MODEL], FP8).ap().rearrange("p (k c) -> p k c", k=32)
    cdiag = sb("cdiag", [128, NTIL_X * DCONV * 128], BF16).ap().rearrange(
        "p (t j c) -> p t j c", t=NTIL_X, j=DCONV)
    disk = sb("disk", [128, NH * 128], BF16).ap().rearrange("p (h c) -> p h c", h=NH)
    uts = sb("uts", [16, HID], BF16).ap()
    yg_f = sb("yg", [128, NH * TB], BF16).ap()
    yg = yg_f.rearrange("p (h t) -> p h t", h=NH)
    xob = sb("xob", [128, 8 * TB], BF16).ap().rearrange("p (a t) -> p a t", a=8)
    rst = sb("rst", [64, NH * 128], F32).ap()
    wring = sb("wring", [128, 3 * 8 * 128], BF16).ap().rearrange(
        "p (s a c) -> p s a c", s=3, a=8)
    woutr = sb("woutr", [128, 2 * 16 * 128], BF16).ap().rearrange(
        "p (s a c) -> p s a c", s=2, a=16)
    onesf = sb("onesf", [128, 128], F32).ap()
    onesb = sb("onesb", [128, 128], BF16).ap()
    idf = sb("idf", [128, 128], F32).ap()
    idb = sb("idb", [128, 128], BF16).ap()
    maskT = sb("maskT", [128, 128], BF16).ap()
    bcs = sb("bcs", [128, TB], BF16).ap()
    tcc = sb("tcc", [64, TB], BF16).ap()
    halo = sb("halo", [128, NTIL_X * 4], BF16).ap().rearrange("p (t k) -> p t k", t=NTIL_X)

    P = Pool2D(nc, "pool", 21272)
    p_dt = P.rect(64, 16, 0, 2048, F32, [16, TB], "dt")
    p_wdec = P.rect(64, 16, 2048, 2048, F32, [16, TB], "wdec")
    p_cs = P.rect(64, 16, 4096, 2048, F32, [16, TB], "cs")
    p_E = P.rect(64, 16, 6144, 2048, F32, [16, TB], "E")
    p_ncs = P.rect(64, 16, 8192, 2048, F32, [16, TB], "ncs")
    p_t1 = P.rect(64, 16, 10240, 2048, F32, [16, TB], "t1")
    p_gate = P.rect(64, 16, 12288, 2048, F32, [16, TB], "gate")
    p_dl = [P.rect(0, 2, 2048 + i * 2048, 2048, F32, [2, TB], f"dl{i}") for i in range(2)]
    p_rr = [P.rect(0, 2, 6144 + i * 2048, 2048, F32, [2, TB], f"rr{i}") for i in range(2)]
    p_er = [P.rect(64, 1, 14336 + i * 1024, 1024, BF16, [1, TB], f"er{i}") for i in range(2)]
    p_erow = P.rect(64, 1, 16384, 256, F32, [1, 64], "erow")
    p_convw = P.rect(0, 128, 16640, 272, F32, [128, NTIL_X, DCONV], "convw")
    p_convb = P.rect(0, 128, 16912, 68, F32, [128, NTIL_X], "convb")
    p_acol = P.rect(64, 16, 16980, 4, F32, [16, 1], "acol")
    p_dtb = P.rect(64, 16, 16984, 4, F32, [16, 1], "dtbias")
    p_gatec = P.rect(0, 16, 16988, 4, F32, [16, 1], "gatecol")
    p_flagc = P.rect(0, 64, 16992, 4, F32, [64, 1], "flagcol")
    p_dtwT = P.rect(0, 128, 16996, 512, F32, [128, NC, 32], "dtwT")
    p_ecols = P.rect(0, 64, 17508, 256, F32, [64, NC, 16], "ecols")
    p_bt = P.rect(0, 128, 17764, 512, BF16, [128, NC, 64], "bt")
    p_cbm = P.rect(0, 128, 18276, 1024, BF16, [128, NC, 128], "cbm")
    p_scale = P.rect(0, 128, 19300, 128, F32, [128, 32], "scale")
    p_recip = P.rect(0, 128, 19428, 128, F32, [128, 32], "recip")
    p_scale2 = P.rect(0, 128, 19556, 32, F32, [128, 8], "scale2")
    p_vb = P.rect(0, 128, 19588, 256, BF16, [128, 8, 16], "vb")
    p_fbt = P.rect(0, 16, 19844, 32, BF16, [16, 16], "fbt")
    p_Eb = P.rect(64, 16, 20224, 1024, BF16, [16, TB], "Eb")
    p_sums = P.rect(0, 128, 21248, 16, F32, [128, 4], "sums")
    p_thr = P.rect(0, 128, 19876, 128, F32, [128, 32], "thr")
    p_nthr = P.rect(0, 128, 20004, 128, F32, [128, 32], "nthr")
    p_thr2 = P.rect(0, 128, 20132, 32, F32, [128, 8], "thr2")
    p_nthr2 = P.rect(0, 128, 20164, 32, F32, [128, 8], "nthr2")
    q_r32 = P.rect(0, 32, 20224, 512, F32, [32, 128], "qr32")
    p_magic = P.rect(0, 128, 21264, 4, F32, [128, 1], "magic")
    p_nmagic = P.rect(0, 128, 21268, 4, F32, [128, 1], "nmagic")

    AR = Pool2D(nc, "arena", 45056)
    a_xbc = [AR.rect(0, 128, i * 2080, 2080, BF16, [128, 1040], f"xbc{i}") for i in range(2)]
    a_xssm = [AR.rect(0, 128, 4160 + i * 1024, 1024, BF16, [128, TB], f"xssm{i}") for i in range(2)]
    a_zsil = [AR.rect(0, 128, 6208 + i * 1024, 1024, BF16, [128, TB], f"zsil{i}") for i in range(2)]
    a_xT = [AR.rect(0, 128, 8256 + i * 1024, 1024, BF16, [128, NC, 128], f"xT{i}") for i in range(2)]
    a_dm = AR.rect(0, 128, 10304, 2048, F32, [128, NC * 128], "dm")
    a_sexp = [AR.rect(0, 128, 12352 + i * 1024, 1024, BF16, [128, NC, 128], f"sexp{i}") for i in range(2)]
    a_m = [AR.rect(0, 128, 14400 + i * 1024, 1024, BF16, [128, NC, 128], f"m{i}") for i in range(2)]
    a_scores = AR.rect(0, 128, 16448, NH * 1024, BF16, [128, NH, NC, 128], "scores")
    a_bdec = [AR.rect(0, 128, 32832 + i * 512, 512, BF16, [128, NC, 64], f"bdec{i}") for i in range(2)]
    a_cexp = [AR.rect(0, 64, 33856 + i * 2048, 2048, F32, [64, NC, 128], f"cexp{i}") for i in range(2)]
    g_s = AR.free_rect(0, 128, 0, 16384, F32, [128, 8, TB], "g_s")
    g_sq = AR.free_rect(0, 128, 16384, 1024, BF16, [128, TB], "g_sq")
    g_tmp = AR.free_rect(0, 128, 18432, 2048, F32, [128, TB], "g_tmp")
    g_x2b = AR.free_rect(0, 128, 20480, 8192, BF16, [128, 8, TB], "g_x2b")
    g_xf = AR.free_rect(0, 128, 28672, 16384, F32, [128, 8, TB], "g_xf")
    g_rs = [AR.free_rect(0, 128, 16384 + i * 1024, 1024, BF16, [128, TB], f"g_rs{i}") for i in range(2)]
    g_r2a_hi = AR.free_rect(0, 128, 28672, 16384, BF16, [128, 16, TB], "g_r2hi")
    g_s1 = AR.free_rect(0, 16, 19456, 1024, BF16, [16, TB], "g_s1")
    g_s2 = AR.free_rect(0, 16, 18432, 1024, BF16, [16, TB], "g_s2")
    p_mrow = AR.free_rect(0, 1, 28672, 2048, F32, [1, TB], "mrow")
    p_rstdr = AR.free_rect(0, 1, 30720, 2048, F32, [1, TB], "rstdrow")
    p_var = AR.free_rect(0, 1, 32768, 2048, F32, [1, TB], "var")
    p_vt1 = AR.free_rect(0, 1, 34816, 2048, F32, [1, TB], "vt1")
    q_ut = AR.free_rect(0, 16, 28672, 16384, F32, [16, HID], "q_ut")
    q_rrow = AR.free_rect(0, 1, 0, 8192, BF16, [1, HID], "q_rrow")

    psA = psm("psA", [128, 512], F32).ap()
    psB = psm("psB", [128, 512], F32).ap()
    psD = psm("psD", [128, 512], F32).ap()
    psC = psm("psC", [128, 512], F32).ap()
    psY = psm("psY", [128, 512], F32).ap()
    psS = psm("psS", [128, 512], F32).ap()
    psT = psm("psT", [128, 1024], BF16).ap()
    psM = psm("psM", [128, 512], F32).ap()

    em = Em(nc)
    for sname in ("s_pe", "s_v", "s_sc", "s_gq", "qsm", "qsg",
                  "qwt0", "qwt1", "qwt2", "qwt3", "qsc",
                  "qrp", "qut", "wr0", "wr1", "wr2", "wo0", "wo1", "qx", "qxf",
                  "qdl0", "qdl1", "qrr0", "qrr1", "qer0", "qer1", "qw0", "qw2",
                  "qw3", "qde", "qerow", "qtc", "qst", "qdbg", "qdbg2", "qon",
                  "qdt"):
        em.sem(sname)

    def vop(fn, wait=()):
        return em.op("vector", fn, wait=wait, inc=[("s_v", 1)])

    def sop(fn, wait=()):
        return em.op("scalar", fn, wait=wait, inc=[("s_sc", 1)])

    def top(fn, wait=()):
        return em.op("tensor", fn, wait=wait, inc=[("s_pe", 1)])

    def gop(fn, wait=()):
        return em.op("gpsimd", fn, wait=wait, inc=[("s_gq", 1)])

    def dsy(sem, fn, wait=()):
        return em.op("sync", fn, wait=wait, inc=[(sem, 16)])

    def dgp(sem, fn, wait=()):
        return em.op("gpsimd", fn, wait=wait, inc=[(sem, 16)])

    ev_xs = dgp("qsc", lambda e: e.dma_start(
        out=x_s, in_=xfm.rearrange("(a p) t -> p a t", p=128)))
    for i in range(4):
        n0, n1 = i * 9, min((i + 1) * 9, 34)
        dgp("qsc", lambda e, n0=n0, n1=n1: e.dma_start(out=win_s[n0:n1],
                                                       in_=win_t[n0:n1]))
    dgp("qsc", lambda e: e.dma_start(out=wout_s, in_=wout_t))
    ev_casts = em.now("qsc")
    gop(lambda e: e.memset(onesf[:], 1.0))
    gop(lambda e: e.memset(idf[:], 0.0))
    gop(lambda e: e.affine_select(out=idf[:], in_=idf[:], compare_op=ALU.not_equal,
                                  fill=1.0, base=0, pattern=[[-1, 128]], channel_multiplier=1))
    gop(lambda e: e.memset(idb[:], 0.0))
    gop(lambda e: e.affine_select(out=idb[:], in_=idb[:], compare_op=ALU.not_equal,
                                  fill=1.0, base=0, pattern=[[-1, 128]], channel_multiplier=1))
    gop(lambda e: e.memset(maskT[:], 1.0))
    gop(lambda e: e.affine_select(out=maskT[:], in_=maskT[:], compare_op=ALU.is_ge,
                                  fill=0.0, base=0, pattern=[[1, 128]], channel_multiplier=-1))
    gop(lambda e: e.memset(p_gate, 1.0))
    gop(lambda e: e.affine_select(out=p_gate.rearrange("p (c t) -> p c t", c=NC),
                                  in_=p_gate.rearrange("p (c t) -> p c t", c=NC),
                                  compare_op=ALU.is_gt, fill=0.0, base=0,
                                  pattern=[[0, NC], [1, CH]], channel_multiplier=0))
    gop(lambda e: e.memset(p_t1, 1.0))
    gop(lambda e: e.memset(halo[:, :, :], 0.0))
    gop(lambda e: e.memset(rst[:], 0.0))
    gop(lambda e: e.memset(p_magic, MAGIC14))
    gop(lambda e: e.memset(p_nmagic, -MAGIC14))
    ev_consts = em.now("s_gq")
    ev_magic = ev_consts
    ev_r0 = ev_consts
    ev_ones = vop(lambda e: e.tensor_copy(onesb[:], onesf[:]), wait=[ev_consts])

    dsy("qsm", lambda e: e.dma_start(out=p_convw, in_=cvw.rearrange("t p k -> p t k")))
    dsy("qsm", lambda e: e.dma_start(out=p_convb, in_=cvb[:]))
    dsy("qsm", lambda e: e.dma_start(out=p_acol, in_=alog[:]))
    dsy("qsm", lambda e: e.dma_start(out=p_dtb, in_=dtbias[:]))
    dsy("qsm", lambda e: e.dma_start(out=p_dl[0][0:1, 0:1], in_=gatei[:]))
    dsy("qsm", lambda e: e.dma_start(out=p_dl[0][0:1, 1:2], in_=flagi[:]))
    dsy("qsm", lambda e: e.dma_start(out=p_dl[0][0:1, 2:18], in_=dskip[:]))
    ev_small = em.now("qsm")
    dgp("qsg", lambda e: e.dma_start(out=p_vb, in_=ffnv[:]), wait=[ev_consts])
    dgp("qsg", lambda e: e.dma_start(out=p_fbt, in_=fbt[:]))
    ev_smgp = em.now("qsg")
    dsy("qon", lambda e: e.dma_start(out=drows[1], in_=p_t1[:, 0:TB]), wait=[ev_consts])

    sop(lambda e: e.activation(out=p_dt[:, 0:1], in_=p_acol, func=AF.Exp),
        wait=[ev_small])
    ev_acol = vop(lambda e: e.tensor_scalar(out=p_acol, in0=p_dt[:, 0:1], scalar1=-1.0,
                                            scalar2=None, op0=ALU.mult),
                  wait=[em.now("s_sc")])
    top(lambda e: e.matmul(psM[0:16, 0:1], onesf[0:1, 0:16], p_dl[0][0:1, 0:1],
                           start=True, stop=True), wait=[ev_small, ev_consts])
    top(lambda e: e.matmul(psM[0:64, 1:2], onesf[0:1, 0:64], p_dl[0][0:1, 1:2],
                           start=True, stop=True))
    top(lambda e: e.matmul(psM[0:128, 2:18], onesf[0:1, 0:128], p_dl[0][0:1, 2:18],
                           start=True, stop=True))
    ev_bc = em.now("s_pe")
    vop(lambda e: e.tensor_copy(p_gatec, psM[0:16, 0:1]), wait=[ev_bc])
    vop(lambda e: e.tensor_copy(p_flagc, psM[0:64, 1:2]))
    for h in range(NH):
        vop(lambda e, h=h: e.tensor_scalar(out=disk[:, h, :], in0=idb[:],
                                           scalar1=psM[0:128, 2 + h:3 + h], scalar2=None,
                                           op0=ALU.mult), wait=[ev_bc, ev_consts])
    ev_disk = em.now("s_v")
    for til in range(NTIL_X):
        for tap in range(DCONV):
            vop(lambda e, til=til, tap=tap: e.tensor_scalar(
                out=cdiag[:, til, tap, :], in0=idb[:],
                scalar1=p_convw[:, til, tap:tap + 1], scalar2=None, op0=ALU.mult),
                wait=[ev_small, ev_consts])
    ev_cdiag = em.now("s_v")

    MAGIC14 = float(np.float32(1.5 * 2 ** 9))
    t_ring = [AR.free_rect(0, 128, 16448 + i * 4096, 4096, F32, [128, 1024],
                           f"tring{i}") for i in range(3)]
    t_hq = [AR.free_rect(0, 128, b, 4096, F32, [128, 1024], f"t_hq{i}")
            for i, b in enumerate((33856, 28736))]
    t_l = [AR.free_rect(0, 128, b, 4096, F32, [128, 1024], f"t_l{i}")
           for i, b in enumerate((37952, 10304))]
    t_out = [AR.free_rect(0, 128, b, 2048, BF16, [128, 1024], f"t_out{i}")
             for i, b in enumerate((14400, 42048))]

    tern_srcs = ([wfc[nt * 128:(nt + 1) * 128, :] for nt in range(32)] +
                 [wpj[nt * 128:(nt + 1) * 128, cc * 1024:(cc + 1) * 1024]
                  for nt in range(8) for cc in range(4)])
    tern_srcs = tern_srcs + tern_srcs
    ring_ev = [None] * 3
    ring_busy = [None] * 3
    ld_next = [0]

    def ensure_loads(upto):
        while ld_next[0] <= min(upto, len(tern_srcs) - 1):
            li = ld_next[0]
            s = li % 3
            ring_ev[s] = dgp(f"qwt{s}", lambda e, s=s, src=tern_srcs[li]: e.dma_start(
                out=t_ring[s][:], in_=src), wait=[ring_busy[s]])
            ld_next[0] += 1

    rd_hq = [None, None]
    rd_l = [None, None]
    drip_q = []
    drip_pending = [None]

    def scale_job(li, cc, nfin, ncols, scol, tcol, ncol):
        def emit():
            ensure_loads(li + 2)
            s = li % 3
            p = li % 2
            ev_abs = sop(lambda e, s=s, p=p: e.activation(out=t_hq[p][:],
                                                          in_=t_ring[s][:],
                                                          func=AF.Abs),
                         wait=[ring_ev[s], ev_magic, rd_hq[p]])
            ring_busy[s] = ev_abs
            sop(lambda e, p=p: e.activation(out=t_l[p][:], in_=t_hq[p][:],
                                            func=AF.Identity, bias=p_magic),
                wait=[rd_l[p]])
            ev_m2 = sop(lambda e, p=p: e.activation(out=t_l[p][:], in_=t_l[p][:],
                                                    func=AF.Identity,
                                                    bias=p_nmagic))
            ev_sub = gop(lambda e, p=p: e.tensor_tensor(
                out=t_hq[p][:], in0=t_hq[p][:], in1=t_l[p][:],
                op=ALU.subtract), wait=[ev_m2])

            def red_part():
                c0, c1 = (0, 1) if nfin == 1 else (2, 3)
                rd_l[p] = vop(lambda e, c0=c0, p=p: e.tensor_reduce(
                    p_sums[:, c0:c0 + 1], t_l[p][:], axis=AX, op=ALU.add),
                    wait=[ev_sub])
                rd_hq[p] = vop(lambda e, c1=c1, p=p: e.tensor_reduce(
                    p_sums[:, c1:c1 + 1], t_hq[p][:], axis=AX, op=ALU.add))
                if nfin > 1:
                    if cc == 0:
                        vop(lambda e: e.tensor_copy(p_sums[:, 0:1], p_sums[:, 2:3]))
                        vop(lambda e: e.tensor_copy(p_sums[:, 1:2], p_sums[:, 3:4]))
                    else:
                        vop(lambda e: e.tensor_tensor(out=p_sums[:, 0:1],
                                                      in0=p_sums[:, 0:1],
                                                      in1=p_sums[:, 2:3], op=ALU.add))
                        vop(lambda e: e.tensor_tensor(out=p_sums[:, 1:2],
                                                      in0=p_sums[:, 1:2],
                                                      in1=p_sums[:, 3:4], op=ALU.add))
                if cc == nfin - 1:
                    vop(lambda e: e.tensor_tensor(out=scol, in0=p_sums[:, 0:1],
                                                  in1=p_sums[:, 1:2], op=ALU.add))
                    vop(lambda e: e.tensor_scalar(out=scol, in0=scol,
                                                  scalar1=1.0 / ncols, scalar2=1e-5,
                                                  op0=ALU.mult, op1=ALU.max))
                    vop(lambda e: e.tensor_scalar(out=tcol, in0=scol, scalar1=0.5,
                                                  scalar2=None, op0=ALU.mult))
                    vop(lambda e: e.tensor_scalar(out=ncol, in0=tcol, scalar1=-1.0,
                                                  scalar2=None, op0=ALU.mult))
            return red_part
        return emit

    tern_slot = [0]
    tern_evac_slot = [None, None]

    def cmp_job(li, tcol, ncol, dst_fn):
        def emit():
            ensure_loads(li + 2)
            s = li % 3
            p = li % 2
            sl = tern_slot[0] % 2
            tern_slot[0] += 1
            sop(lambda e, s=s, p=p: e.activation(out=t_hq[p][:], in_=t_ring[s][:],
                                                 func=AF.Sign, bias=ncol),
                wait=[ring_ev[s], rd_hq[p]])
            ev_s2 = sop(lambda e, s=s, p=p: e.activation(out=t_l[p][:],
                                                         in_=t_ring[s][:],
                                                         func=AF.Sign, bias=tcol),
                        wait=[rd_l[p]])
            ring_busy[s] = ev_s2
            ev_o = gop(lambda e, sl=sl, p=p: e.tensor_tensor(
                out=t_out[sl][:], in0=t_hq[p][:], in1=t_l[p][:], op=ALU.add),
                wait=[ev_s2, tern_evac_slot[sl]])
            rd_hq[p] = ev_o
            rd_l[p] = ev_o

            def pe_part():
                pbank = (psD if sl == 0 else psC).bitcast(BF16)
                for kc in range(8):
                    top(lambda e, kc=kc, pbank=pbank: e.transpose(
                        pbank[:, kc * 128:(kc + 1) * 128],
                        t_out[sl][:, kc * 128:(kc + 1) * 128], idb[:]),
                        wait=[ev_o, ev_consts, tern_evac_slot[sl]])
                evt = em.now("s_pe")
                tern_evac_slot[sl] = vop(lambda e, pbank=pbank: e.tensor_scalar(
                    out=dst_fn(),
                    in0=pbank[:, 0:1024].rearrange("p (kc c) -> p kc c", kc=8),
                    scalar1=0.5, scalar2=None, op0=ALU.mult), wait=[evt])
            return pe_part
        return emit

    def drip_flush(n=2):
        for _ in range(n):
            if drip_pending[0]:
                drip_pending[0]()
                drip_pending[0] = None
            if drip_q:
                drip_pending[0] = drip_q.pop(0)()

    def drip_drain():
        while drip_q or drip_pending[0]:
            drip_flush(1)

    ev_recip = None
    for nt in range(32):
        drip_q.append(scale_job(nt, 0, 1, 1024, p_scale[:, nt:nt + 1],
                                p_thr[:, nt:nt + 1], p_nthr[:, nt:nt + 1]))

    def recip_job():
        nonlocal ev_recip
        ev_recip = vop(lambda e: e.reciprocal(p_recip, p_scale))
    drip_q.append(recip_job)
    for nt in range(8):
        for cc in range(4):
            drip_q.append(scale_job(32 + nt * 4 + cc, cc, 4, 4096,
                                    p_scale2[:, nt:nt + 1], p_thr2[:, nt:nt + 1],
                                    p_nthr2[:, nt:nt + 1]))
    for nt in range(32):
        drip_q.append(cmp_job(64 + nt, p_thr[:, nt:nt + 1], p_nthr[:, nt:nt + 1],
                              (lambda nt=nt: fc8[:, :, nt * 128:(nt + 1) * 128])))
    for nt in range(8):
        for cc in range(4):
            drip_q.append(cmp_job(
                96 + nt * 4 + cc, p_thr2[:, nt:nt + 1], p_nthr2[:, nt:nt + 1],
                (lambda nt=nt, cc=cc: pj8[:, cc * 8:(cc + 1) * 8,
                                          nt * 128:(nt + 1) * 128])))

    ev_uts = None

    def uts_build():
        nonlocal ev_uts
        top(lambda e: e.transpose(psM[0:32, 0:128], p_recip, idf[:]),
            wait=[ev_recip, ev_consts, em.now("s_v")])
        vop(lambda e: e.tensor_copy(q_r32, psM[0:32, 0:128]), wait=[em.now("s_pe")])
        ev1 = dsy("qut", lambda e: e.dma_start(out=dsc[:], in_=q_r32),
                  wait=[em.now("s_v")])
        ev2 = dgp("qrp", lambda e: e.dma_start(
            out=q_rrow, in_=dsc.rearrange("a b -> (a b)").unsqueeze(0)), wait=[ev1])
        ev3 = dsy("qsm", lambda e: e.dma_start(out=q_ut, in_=ffnu[:]),
                  wait=[em.now("s_v"), em.now("s_pe")])
        for j in range(8):
            top(lambda e, j=j: e.matmul(psM[0:16, 0:512], onesb[0:1, 0:16],
                                        q_rrow[0:1, j * 512:(j + 1) * 512],
                                        start=True, stop=True),
                wait=[ev2, ev3, ev_ones] + ([ev_uts] if ev_uts else []))
            evb = em.now("s_pe")
            ev_uts = vop(lambda e, j=j: e.tensor_tensor(
                out=uts[:, j * 512:(j + 1) * 512],
                in0=q_ut[:, j * 512:(j + 1) * 512],
                in1=psM[0:16, 0:512], op=ALU.mult), wait=[evb])

    prev_p3_v = em.now("s_v")
    prev_p3_pe = em.now("s_pe")
    prev_H_pe = em.now("s_pe")
    prev_s2 = em.now("s_v")
    prev_f1rs = em.now("s_sc")
    prev_store = None
    prev_ew = None
    prev_ebc = None
    prev_erow_rd = None
    prev_w0 = None
    prev_w2 = None
    prev_w3 = None
    prev_tc = None
    prev_dl_rd = []
    prev_rr_rd = []
    prev_er_rd = []
    ev_prev_states = None
    ev_flagged = None

    wslot = [0]
    wslot_busy = [None] * 3

    def wtile_load(ct):
        s = wslot[0] % 3
        wslot[0] += 1
        w = [wslot_busy[s], ev_casts] if wslot_busy[s] is not None else [ev_casts]
        ev = dsy(f"wr{s}", lambda e, ct=ct, s=s: e.dma_start(out=wring[:, s, :, :],
                                                             in_=win_s[ct]), wait=w)
        return s, ev

    s_dt0, ev_wdt0 = wtile_load(33)
    ev_dtprev = None
    ev_dtst = None
    for b in range(NB):
        evx = dsy("qx", lambda e, b=b: e.dma_start(
            out=xob[:, :, :], in_=x_s[:, :, b * TB:(b + 1) * TB]),
            wait=[ev_casts, ev_dtprev])
        for kc in range(8):
            top(lambda e, kc=kc, s_dt0=s_dt0: e.matmul(
                psM[64:80, 0:TB], wring[:, s_dt0, kc, 0:16], xob[:, kc, :],
                start=(kc == 0), stop=(kc == 7)),
                wait=[ev_wdt0, evx, em.now("s_v")])
        ev_dtprev = em.now("s_pe")
        vop(lambda e: e.tensor_copy(p_t1, psM[64:80, 0:TB]),
            wait=[ev_dtprev, ev_dtst, ("qon", 16)])
        ev_dtst = dsy("qdt", lambda e, b=b: e.dma_start(
            out=dt_all[:, b * TB:(b + 1) * TB], in_=p_t1), wait=[em.now("s_v")])
    wslot_busy[s_dt0] = ev_dtprev
    ev_dt_all = em.now("qdt")


    prev_xob_rd = em.now("s_pe")
    pending_x = None
    prev_peb = None
    prev_dtwmm = [None]

    def p0_chain(blk):
        nonlocal prev_ew, prev_ebc, prev_erow_rd, prev_w0, prev_w2, prev_w3
        nonlocal prev_peb
        tok0 = blk * TB
        evr = dsy("qdt", lambda e, tok0=tok0: e.dma_start(
            out=p_E, in_=dt_all[:, tok0:tok0 + TB]),
            wait=[ev_dt_all, prev_peb])
        sop(lambda e: e.activation(out=p_t1, in_=p_E, func=AF.Exp, bias=p_dtb),
            wait=[evr, ev_small, prev_ew, ("qon", 16)])
        ev_dt = sop(lambda e: e.activation(out=p_dt, in_=p_t1, func=AF.Ln,
                                           bias=1.0), wait=[prev_dtwmm[0]])
        ev_da = vop(lambda e: e.tensor_scalar(out=p_t1, in0=p_dt, scalar1=p_acol,
                                              scalar2=None, op0=ALU.mult),
                    wait=[ev_dt, ev_acol])
        ev_cs = vop(lambda e: e.tensor_tensor_scan(p_cs, p_gate, p_t1, 0.0,
                                                   op0=ALU.mult, op1=ALU.add),
                    wait=[ev_da, ev_consts, prev_w2])
        ev_E = sop(lambda e: e.activation(out=p_E, in_=p_cs, func=AF.Exp),
                   wait=[ev_cs, prev_w3])
        ev_ncs = vop(lambda e: e.tensor_scalar(out=p_ncs, in0=p_cs, scalar1=-1.0,
                                               scalar2=None, op0=ALU.mult),
                     wait=[ev_cs, prev_w0])
        ev_wd = []
        for c in range(NC):
            vop(lambda e, c=c: e.tensor_scalar(
                out=p_wdec[:, c * CH:(c + 1) * CH], in0=p_cs[:, c * CH:(c + 1) * CH],
                scalar1=p_cs[:, c * CH + CH - 1:c * CH + CH], scalar2=None,
                op0=ALU.subtract), wait=[ev_cs, prev_dtwmm[0]])
            sop(lambda e, c=c: e.activation(out=p_wdec[:, c * CH:(c + 1) * CH],
                                            in_=p_wdec[:, c * CH:(c + 1) * CH],
                                            func=AF.Exp, scale=-1.0), wait=[em.now("s_v")])
            ev_wd.append(vop(lambda e, c=c: e.tensor_tensor(
                out=p_wdec[:, c * CH:(c + 1) * CH], in0=p_wdec[:, c * CH:(c + 1) * CH],
                in1=p_dt[:, c * CH:(c + 1) * CH], op=ALU.mult), wait=[em.now("s_sc")]))
        ev_ec = sop(lambda e: e.activation(out=p_t1[:, 0:NC], in_=p_cs[:, CH - 1:TB:CH],
                                           func=AF.Exp), wait=[ev_cs, ev_da])
        ev_ew = dsy("qde", lambda e: e.dma_start(out=de[:], in_=p_t1[:, 0:NC]),
                    wait=[ev_ec, prev_erow_rd])
        ev_er = dsy("qerow", lambda e: e.dma_start(
            out=p_erow.rearrange("p (c h) -> p c h", c=NC),
            in_=de.rearrange("h c -> c h").unsqueeze(0),
            allow_slow_non_contiguous=True),
            wait=[ev_ew, prev_ebc])
        w_rows = [ev_ncs] + prev_dl_rd
        w_rows2 = list(prev_dl_rd) + list(prev_rr_rd)
        w_rows3 = list(prev_er_rd)
        if prev_store:
            w_rows.append(prev_store)
        ev_w0 = dsy("qw0", lambda e: e.dma_start(out=drows[0], in_=p_ncs), wait=w_rows)
        ev_w2 = dsy("qw2", lambda e: e.dma_start(out=drows[2], in_=p_cs),
                    wait=[ev_cs] + w_rows2)
        ev_eb = vop(lambda e: e.tensor_copy(p_Eb, p_E), wait=[ev_E])
        ev_w3 = dsy("qw3", lambda e: e.dma_start(out=drows3b, in_=p_Eb),
                    wait=[ev_eb] + w_rows3)
        prev_ew = ev_ew
        prev_erow_rd = ev_er
        prev_peb = ev_eb
        prev_w0, prev_w2, prev_w3 = ev_w0, ev_w2, ev_w3
        return dict(ev_wd=ev_wd, ev_er=ev_er, ev_w0=ev_w0, ev_w2=ev_w2,
                    ev_w3=ev_w3)

    ch = p0_chain(0)
    ch_pending = None
    for blk in range(NB):
        own = blk >= 4
        tok0 = blk * TB
        if blk > 0 and ch_pending is not None:
            ch = ch_pending
            ch_pending = None
        elif blk > 0:
            prev_dl_rd = [em.now("qdl0"), em.now("qdl1")]
            prev_rr_rd = [em.now("qrr0"), em.now("qrr1")]
            prev_er_rd = [em.now("qer0"), em.now("qer1")]
            ch = p0_chain(blk)
        if pending_x is None:
            ev_x = dsy("qx", lambda e, tok0=tok0: e.dma_start(
                out=xob[:, :, :], in_=x_s[:, :, tok0:tok0 + TB]),
                wait=[prev_xob_rd, ev_casts])
        else:
            ev_x = pending_x

        ev_w0, ev_w2, ev_w3 = ch["ev_w0"], ch["ev_w2"], ch["ev_w3"]
        for c in range(NC):
            top(lambda e, c=c: e.matmul(psM[0:64, 128 + c * 16:128 + (c + 1) * 16],
                                        onesf[64:65, 0:64],
                                        p_erow[0:1, c * 16:(c + 1) * 16],
                                        start=True, stop=True),
                wait=[ch["ev_er"], prev_s2])
        ev_ebc = em.now("s_pe")
        prev_ebc = ev_ebc
        ev_ecols = vop(lambda e: e.tensor_copy(
            p_ecols, psM[0:64, 128:128 + NC * 16].rearrange("p (c k) -> p c k", c=NC)),
            wait=[ev_ebc])
        for c in range(NC):
            top(lambda e, c=c: e.transpose(psM[0:128, 256 + c * 16:256 + (c + 1) * 16],
                                           p_dt[:, c * CH:(c + 1) * CH],
                                           idf[64:80, 64:80]),
                wait=[ch["ev_wd"][c], ev_consts, ev_ecols])
            top(lambda e, c=c: e.transpose(psM[0:128, 320 + c * 16:320 + (c + 1) * 16],
                                           p_wdec[:, c * CH:(c + 1) * CH],
                                           idf[64:80, 64:80]),
                wait=[ch["ev_wd"][c], ev_consts, ev_ecols])
        ev_dtwmm = em.now("s_pe")
        prev_dtwmm[0] = ev_dtwmm
        vop(lambda e: e.tensor_copy(
            p_dtwT[:, :, 0:16],
            psM[0:128, 256:256 + NC * 16].rearrange("p (c k) -> p c k", c=NC)),
            wait=[ev_dtwmm])
        ev_dtwT = vop(lambda e: e.tensor_copy(
            p_dtwT[:, :, 16:32],
            psM[0:128, 320:320 + NC * 16].rearrange("p (c k) -> p c k", c=NC)))

        s_bc, ev_wbc = wtile_load(32)
        for kc in range(8):
            top(lambda e, s_bc=s_bc, kc=kc: e.matmul(
                psA[:, 0:TB], wring[:, s_bc, kc, :], xob[:, kc, :],
                start=(kc == 0), stop=(kc == 7)),
                wait=[ev_wbc, ev_x, prev_f1rs])
        ev_bcmm = em.now("s_pe")
        wslot_busy[s_bc] = ev_bcmm
        xbcb = a_xbc[0]
        w_ev = [ev_bcmm, prev_p3_v]
        if prev_store:
            w_ev.append(prev_store)
        vop(lambda e: e.tensor_copy(xbcb[:, 0:3], halo[:, 16, 0:3]), wait=w_ev)
        vop(lambda e: e.tensor_copy(xbcb[:, 3:3 + TB], psA[:, 0:TB]))
        vop(lambda e: e.tensor_copy(halo[:, 16, 0:3], psA[:, TB - 3:TB]))
        ev_xbc_bc = em.now("s_v")
        for tap in range(DCONV):
            top(lambda e, tap=tap: e.matmul(
                psB[:, 0:TB], cdiag[:, 16, tap, :], xbcb[:, tap:tap + TB],
                start=(tap == 0), stop=(tap == 3)), wait=[ev_xbc_bc, ev_cdiag])
        ev_convbc = em.now("s_pe")
        ev_bcs = sop(lambda e: e.activation(out=bcs[:], in_=psB[:, 0:TB], func=AF.Silu,
                                            bias=p_convb[:, 16:17]),
                     wait=[ev_convbc, ev_small, prev_tc])
        ev_tc = dsy("qtc", lambda e: e.dma_start(out=tcc[:], in_=bcs[64:128, :]),
                    wait=[ev_bcs, prev_p3_v, prev_p3_pe])
        prev_tc = ev_tc
        for c in range(NC):
            top(lambda e, c=c: e.transpose(psT[:, 512 + c * 64:512 + (c + 1) * 64],
                                           bcs[0:64, c * CH:(c + 1) * CH],
                                           idb[0:64, 0:64]),
                wait=[ev_bcs, ev_convbc])
        ev_btmm = em.now("s_pe")
        ev_bt = vop(lambda e: e.tensor_copy(
            p_bt, psT[:, 512:512 + NC * 64].rearrange("p (c k) -> p c k", c=NC)),
            wait=[ev_btmm])
        ev_cbm = None
        if own:
            tg = [e for e in tern_evac_slot if e] if blk == 4 else []
            for c in range(NC):
                top(lambda e, c=c: e.matmul(
                    psD[:, c * 128:(c + 1) * 128], bcs[0:64, c * CH:(c + 1) * CH],
                    tcc[:, c * CH:(c + 1) * CH], start=True, stop=True),
                    wait=[ev_tc, ev_bcs] + tg)
            ev_cbmm = em.now("s_pe")
            ev_cbm = vop(lambda e: e.tensor_tensor(
                out=p_cbm, in0=psD[:, 0:512].rearrange("p (c k) -> p c k", c=NC),
                in1=maskT[:].unsqueeze(1).broadcast_to((128, NC, 128)), op=ALU.mult),
                wait=[ev_cbmm, ev_consts])

        if own:
            ev_dl_used = [None, None]
            last_min = None
            ev_exp_par = [None, None]
            last_S = [None, None]
            for h in range(NH):
                pp = h % 2
                wrow = [ev_w0, ev_w2, ("qon", 16)]
                if ev_dl_used[pp] is not None:
                    wrow.append(ev_dl_used[pp])
                evd1 = dsy(f"qdl{pp}", lambda e, h=h, pp=pp: e.dma_start(
                    out=p_dl[pp], in_=drows[0:2, h, :]), wait=wrow)
                evd2 = dsy(f"qrr{pp}", lambda e, h=h, pp=pp: e.dma_start(
                    out=p_rr[pp], in_=drows[1:3, h, :]),
                    wait=[ev_w2, ("qon", 16)] + ([ev_dl_used[pp]] if ev_dl_used[pp] else []))
                for c in range(NC):
                    top(lambda e, pp=pp, c=c: e.matmul(
                        psD[:, c * 128:(c + 1) * 128],
                        p_dl[pp][:, c * CH:(c + 1) * CH],
                        p_rr[pp][:, c * CH:(c + 1) * CH], start=True, stop=True),
                        wait=[evd1, evd2, ev_cbm if h == 0 else last_min])
                ev_dmm = em.now("s_pe")
                ev_dl_used[pp] = ev_dmm
                last_min = vop(lambda e: e.tensor_scalar(
                    out=a_dm, in0=psD[:, 0:512], scalar1=0.0, scalar2=None,
                    op0=ALU.min), wait=[ev_dmm, ev_exp_par[pp] if h >= 2 else None])
                w_exp = [last_min]
                if last_S[pp] is not None:
                    w_exp.append(last_S[pp])
                ev_exp = sop(lambda e, pp=pp: e.activation(
                    out=a_sexp[pp][:, :, :], in_=a_dm.rearrange("p (c k) -> p c k", c=NC),
                    func=AF.Exp), wait=w_exp)
                ev_exp_par[pp] = ev_exp
                for c in range(NC):
                    vop(lambda e, h=h, pp=pp, c=c: e.scalar_tensor_tensor(
                        out=a_m[pp][:, c, :], in0=p_cbm[:, c, :],
                        scalar=p_dtwT[:, c, h:h + 1], in1=disk[:, h, :],
                        op0=ALU.mult, op1=ALU.add),
                        wait=[ev_cbm, ev_dtwT, ev_disk])
                last_S[pp] = vop(lambda e, h=h, pp=pp: e.tensor_tensor(
                    out=a_scores[:, h, :, :], in0=a_sexp[pp][:, :, :],
                    in1=a_m[pp][:, :, :], op=ALU.mult), wait=[ev_exp])
            ev_scores = em.now("s_v")

        st = [dict() for _ in range(NH)]
        xob_rd = [em.now("s_pe")]
        ev_conv_done = None
        ev_xT_ev = None
        ev_cexp_ev = None
        ev_er_used = [None, None]
        last_y = None
        last_sil = None
        last_zsil = None
        last_xbc_evac = None

        def stage_a1(h):
            nonlocal last_xbc_evac, ev_conv_done
            d = st[h]
            d["pp"] = h % 2
            s_x, ev_wx = wtile_load(16 + h)
            if own:
                d["s_z"], d["ev_wz"] = wtile_load(h)
                wE = [ev_w3]
                if ev_er_used[h % 2] is not None:
                    wE.append(ev_er_used[h % 2])
                d["ev_erh"] = dsy(f"qer{h % 2}", lambda e, h=h: e.dma_start(
                    out=p_er[h % 2], in_=drows3b[h, :].unsqueeze(0)), wait=wE)
            w_inp = [ev_wx, ev_x]
            if h == 0:
                w_inp += [ev_xbc_bc, ev_bcmm]
            else:
                w_inp += [last_zsil if own else last_xbc_evac]
            for kc in range(8):
                top(lambda e, s_x=s_x, kc=kc: e.matmul(
                    psA[:, 0:TB], wring[:, s_x, kc, :], xob[:, kc, :],
                    start=(kc == 0), stop=(kc == 7)), wait=w_inp)
            ev_xmm = em.now("s_pe")
            wslot_busy[s_x] = ev_xmm
            xob_rd[0] = ev_xmm
            w_ev = [ev_xmm]
            if ev_conv_done is not None:
                w_ev.append(ev_conv_done)
            xbch = a_xbc[1]
            vop(lambda e, h=h: e.tensor_copy(xbch[:, 0:3], halo[:, h, 0:3]), wait=w_ev)
            vop(lambda e: e.tensor_copy(xbch[:, 3:3 + TB], psA[:, 0:TB]))
            vop(lambda e, h=h: e.tensor_copy(halo[:, h, 0:3], psA[:, TB - 3:TB]))
            d["ev_xbc"] = em.now("s_v")
            last_xbc_evac = d["ev_xbc"]

        def stage_a2(h):
            nonlocal ev_conv_done, last_sil, last_zsil
            d = st[h]
            pp = d["pp"]
            xbch = a_xbc[1]
            w_cv = [d["ev_xbc"], last_sil if h > 0 else ev_bcs]
            for tap in range(DCONV):
                top(lambda e, h=h, tap=tap: e.matmul(
                    psB[:, 0:TB], cdiag[:, h, tap, :], xbch[:, tap:tap + TB],
                    start=(tap == 0), stop=(tap == 3)), wait=w_cv)
            ev_conv_done = em.now("s_pe")
            d["ev_sil"] = sop(lambda e, h=h, pp=pp: e.activation(
                out=a_xssm[pp][:], in_=psB[:, 0:TB], func=AF.Silu,
                bias=p_convb[:, h:h + 1]), wait=[ev_conv_done, ev_xT_ev])
            last_sil = d["ev_sil"]
            if own:
                for kc in range(8):
                    top(lambda e, d=d, kc=kc: e.matmul(
                        psA[:, 0:TB], wring[:, d["s_z"], kc, :], xob[:, kc, :],
                        start=(kc == 0), stop=(kc == 7)),
                        wait=[d["ev_wz"], d["ev_xbc"]])
                ev_zmm = em.now("s_pe")
                wslot_busy[d["s_z"]] = ev_zmm
                xob_rd[0] = ev_zmm
                d["ev_zsil"] = sop(lambda e, pp=pp: e.activation(
                    out=a_zsil[pp][:], in_=psA[:, 0:TB], func=AF.Silu),
                    wait=[ev_zmm, last_y if h >= 2 else None])
                last_zsil = d["ev_zsil"]

        def stage_b(h):
            nonlocal ev_xT_ev, ev_cexp_ev, last_y, ev_prev_states
            d = st[h]
            pp = d["pp"]
            w_xt = [d["ev_sil"], ev_xT_ev if h > 0 else ev_bt]
            for c in range(NC):
                top(lambda e, pp=pp, c=c: e.transpose(
                    psT[:, c * 128:(c + 1) * 128],
                    a_xssm[pp][:, c * CH:(c + 1) * CH], idb[:]), wait=w_xt)
            ev_xTmm = em.now("s_pe")
            ev_xT_ev = vop(lambda e, pp=pp: e.tensor_copy(
                a_xT[pp][:, :, :], psT[:, 0:512].rearrange("p (c k) -> p c k", c=NC)),
                wait=[ev_xTmm])
            ev_bd = vop(lambda e, h=h, pp=pp: e.tensor_tensor(
                out=a_bdec[pp][:, :, :], in0=p_bt[:, :, :],
                in1=p_dtwT[:, :, 16 + h:17 + h].broadcast_to((128, NC, 64)),
                op=ALU.mult), wait=[ev_bt, ev_dtwT, em.now("s_pe")])
            if own:
                w_cb = [d["ev_erh"], ev_cexp_ev if ev_cexp_ev else ev_cbmm]
                if blk == 4 and h == 0:
                    w_cb += [e for e in tern_evac_slot if e]
                for c in range(NC):
                    top(lambda e, pp=pp, c=c: e.matmul(
                        psC[0:64, c * 128:(c + 1) * 128], onesb[64:65, 0:64],
                        p_er[pp][0:1, c * CH:(c + 1) * CH], start=True, stop=True),
                        wait=w_cb)
                ev_cbc = em.now("s_pe")
                ev_er_used[pp] = ev_cbc
                w_cx = [ev_cbc, ev_tc]
                if h == 0 and prev_store:
                    w_cx.append(prev_store)
                ev_cexp_ev = vop(lambda e, pp=pp: e.tensor_tensor(
                    out=a_cexp[pp][:, :, :],
                    in0=tcc[:].rearrange("p (c k) -> p c k", c=NC),
                    in1=psC[0:64, 0:512].rearrange("p (c k) -> p c k", c=NC),
                    op=ALU.mult), wait=w_cx)
            for c in range(NC):
                wst = [ev_xT_ev, ev_bd]
                if ev_prev_states is not None:
                    wst.append(ev_prev_states)
                top(lambda e, pp=pp, c=c: e.matmul(
                    psS[0:64, c * 128:(c + 1) * 128], a_bdec[pp][:, c, :],
                    a_xT[pp][:, c, :], start=True, stop=True), wait=wst)
                if own:
                    w_yd = [ev_scores, ev_xT_ev]
                    if last_y is not None:
                        w_yd.append(last_y)
                    top(lambda e, h=h, pp=pp, c=c: e.matmul(
                        psY[:, c * 128:(c + 1) * 128], a_xT[pp][:, c, :],
                        a_scores[:, h, c, :], start=True, stop=False), wait=w_yd)
                    w_yo = [ev_cexp_ev, ev_r0]
                    if ev_flagged is not None and blk == 4:
                        w_yo.append(ev_flagged)
                    top(lambda e, h=h, pp=pp, c=c: e.matmul(
                        psY[:, c * 128:(c + 1) * 128], rst[:, h * 128:(h + 1) * 128],
                        a_cexp[pp][:, c, :], start=False, stop=True), wait=w_yo)
                ev_y_c = em.now("s_pe")
                ev_prev_states = vop(lambda e, h=h, c=c: e.scalar_tensor_tensor(
                    out=rst[:, h * 128:(h + 1) * 128],
                    in0=rst[:, h * 128:(h + 1) * 128],
                    scalar=p_ecols[:, c, h:h + 1],
                    in1=psS[0:64, c * 128:(c + 1) * 128],
                    op0=ALU.mult, op1=ALU.add),
                    wait=[ev_y_c, ev_ecols, ev_r0])
            if own:
                last_y = vop(lambda e, h=h, pp=pp: e.tensor_tensor(
                    out=yg[:, h, :], in0=psY[:, 0:512], in1=a_zsil[pp][:],
                    op=ALU.mult), wait=[em.now("s_pe"), d["ev_zsil"]])

        stage_a1(0)
        stage_a2(0)
        if not own:
            drip_flush(1)
        for h in range(1, NH):
            stage_a1(h)
            stage_b(h - 1)
            if not own:
                drip_flush(1)
            stage_a2(h)
            if not own:
                drip_flush(1)
        stage_b(NH - 1)
        if blk == 3:
            drip_drain()
            uts_build()
            prev_s2 = ev_uts
        if blk + 1 < NB:
            pending_x = dsy("qx", lambda e, tok0=tok0 + TB: e.dma_start(
                out=xob[:, :, :], in_=x_s[:, :, tok0:tok0 + TB]),
                wait=[xob_rd[0], ev_casts])
        ev_H_v = em.now("s_v")
        ev_H_pe = em.now("s_pe")
        prev_H_pe = ev_H_pe
        prev_xob_rd = xob_rd[0]

        if blk == 3:
            rtmp = g_s.tensor.ap().bitcast(F32)[0:64, 0:2048]
            vop(lambda e: e.tensor_scalar(out=rtmp, in0=rst[:], scalar1=p_flagc,
                                          scalar2=None, op0=ALU.mult),
                wait=[ev_H_v, ev_H_pe])
            ev_flagged = vop(lambda e: e.tensor_copy(rst[:], rtmp))

        if dbg and blk == 4:
            dsy("qdbg", lambda e: e.dma_start(out=dbg_t["d_dt"][:], in_=p_dt), wait=[ev_H_v])
            dsy("qdbg", lambda e: e.dma_start(out=dbg_t["d_cs"][:], in_=p_cs), wait=[ev_H_v])
            vop(lambda e: e.tensor_copy(g_tmp[:, 0:TB], a_xssm[(NH - 1) % 2][:]),
                wait=[ev_H_v, ev_H_pe])
            dsy("qdbg", lambda e: e.dma_start(out=dbg_t["d_xssm"][:], in_=g_tmp[:, 0:TB]),
                wait=[em.now("s_v")])
            vop(lambda e: e.tensor_copy(g_tmp[:, 0:TB], a_scores[:, 0, :, :]),
                wait=[em.now("qdbg")])
            dsy("qdbg", lambda e: e.dma_start(out=dbg_t["d_scores"][:], in_=g_tmp[:, 0:TB]),
                wait=[em.now("s_v")])
            rtmp2 = g_s.tensor.ap().bitcast(F32)[0:64, 0:2048]
            vop(lambda e: e.tensor_copy(rtmp2, rst[:]), wait=[em.now("qdbg")])
            dsy("qdbg", lambda e: e.dma_start(out=dbg_t["d_r"][:], in_=rtmp2),
                wait=[em.now("s_v")])
            vop(lambda e: e.tensor_copy(g_tmp[:, 0:512].bitcast(BF16)[:, 0:1024],
                                        fc8[:, 0, 0:1024]), wait=[em.now("qdbg")])
            dgp("qdbg2", lambda e: e.dma_start(out=dbg_t["d_fc8"][:],
                                      in_=g_tmp[:, 0:512].bitcast(BF16)[:, 0:1024]),
                wait=[em.now("s_v")])
            vop(lambda e: e.tensor_copy(g_tmp[:, 0:32], p_scale2[:, 0:8]), wait=[em.now("qdbg2")]) if False else None
            vop(lambda e: e.tensor_copy(g_tmp[:, 0:8], p_scale2), wait=[em.now("qdbg2")])
            dsy("qdbg", lambda e: e.dma_start(out=dbg_t["d_scale"][:, 0:8], in_=g_tmp[:, 0:8]),
                wait=[em.now("s_v")])
            for pk in range(16):
                vop(lambda e, pk=pk: e.tensor_copy(
                    g_tmp[:, 0:512].bitcast(BF16)[:, 0:1024] if False else g_tmp[:].bitcast(BF16)[:, 0:1024],
                    pj8.tensor.ap().bitcast(FP8)[:, pk * 2048:(pk + 1) * 2048][:, 0:1024] if False else
                    pj8[:, pk * 2, :].rearrange("p c -> p c") if False else
                    pj8.tensor.ap()[:, pk * 2048:pk * 2048 + 1024]),
                    wait=[em.now("qdbg2")])
                dgp("qdbg2", lambda e, pk=pk: e.dma_start(
                    out=dbg_t["d_pj8"][:, pk * 2048:pk * 2048 + 1024],
                    in_=g_tmp[:].bitcast(BF16)[:, 0:1024]), wait=[em.now("s_v")])
                vop(lambda e, pk=pk: e.tensor_copy(
                    g_tmp[:].bitcast(BF16)[:, 0:1024],
                    pj8.tensor.ap()[:, pk * 2048 + 1024:pk * 2048 + 2048]),
                    wait=[em.now("qdbg2")])
                dgp("qdbg2", lambda e, pk=pk: e.dma_start(
                    out=dbg_t["d_pj8"][:, pk * 2048 + 1024:pk * 2048 + 2048],
                    in_=g_tmp[:].bitcast(BF16)[:, 0:1024]), wait=[em.now("s_v")])

        if not own:
            prev_p3_v = em.now("s_v")
            prev_p3_pe = ev_H_pe
            continue

        ev_xf = dsy("qxf", lambda e, tok0=tok0: e.dma_start(
            out=g_xf[:, :, :],
            in_=xfm.rearrange("(a p) t -> p a t", p=128)[:, :, tok0:tok0 + TB]),
            wait=[ev_H_v, ev_H_pe, prev_store])
        ev_wo_used = [None, None]
        ev_s_tiles = []
        for dt_ in range(8):
            sl = dt_ % 2
            w = [ev_H_pe]
            if ev_wo_used[sl] is not None:
                w.append(ev_wo_used[sl])
            ev_wo = dsy(f"wo{sl}", lambda e, dt_=dt_, sl=sl: e.dma_start(
                out=woutr[:, sl, :, :], in_=wout_s[dt_]), wait=w)
            pps = psA if sl == 0 else psB
            w_om = [ev_wo, ev_H_v]
            if dt_ >= 2:
                w_om.append(ev_s_tiles[dt_ - 2])
            for kc in range(16):
                top(lambda e, sl=sl, kc=kc, pps=pps: e.matmul(
                    pps[:, 0:TB], woutr[:, sl, kc, :], yg[:, kc, :],
                    start=(kc == 0), stop=(kc == 15)), wait=w_om)
            ev_omm = em.now("s_pe")
            ev_wo_used[sl] = ev_omm
            ev_s_tiles.append(vop(lambda e, dt_=dt_, pps=pps: e.tensor_tensor(
                out=g_s[:, dt_, :], in0=pps[:, 0:TB], in1=g_xf[:, dt_, :],
                op=ALU.add), wait=[ev_omm, ev_xf]))

        def layer_norm(src_ap, out_ap, ev_tiles):
            for i in range(8):
                top(lambda e, i=i: e.matmul(psD[0:1, 0:TB], onesf[:, 0:1],
                                            src_ap[:, i, :],
                                            start=(i == 0), stop=(i == 7)),
                    wait=(list(ev_tiles) if i == 0 else [ev_tiles[i]]) + [ev_consts])
            ev_mean = em.now("s_pe")
            ev_sq = None
            for i in range(8):
                w_sq = [ev_tiles[i]]
                if ev_sq is not None:
                    w_sq.append(ev_sq)
                evq = sop(lambda e, i=i: e.activation(out=g_sq[:], in_=src_ap[:, i, :],
                                                      func=AF.Square), wait=w_sq)
                ev_sq = top(lambda e, i=i: e.matmul(psC[0:1, 0:TB], onesb[:, 0:1],
                                                    g_sq[:], start=(i == 0),
                                                    stop=(i == 7)),
                            wait=[evq] + (list(ev_tiles) if i == 0 else []))
            ev_sqm = em.now("s_pe")
            vop(lambda e: e.tensor_scalar(out=p_mrow, in0=psD[0:1, 0:TB],
                                          scalar1=1.0 / 1024.0, scalar2=None,
                                          op0=ALU.mult), wait=[ev_mean, ev_sqm])
            vop(lambda e: e.tensor_scalar(out=p_var, in0=psC[0:1, 0:TB],
                                          scalar1=1.0 / 1024.0, scalar2=None,
                                          op0=ALU.mult))
            vop(lambda e: e.tensor_tensor(out=p_vt1, in0=p_mrow, in1=p_mrow,
                                          op=ALU.mult))
            vop(lambda e: e.tensor_tensor(out=p_var, in0=p_var, in1=p_vt1,
                                          op=ALU.subtract))
            ev_v = vop(lambda e: e.tensor_scalar(out=p_var, in0=p_var, scalar1=EPS,
                                                 scalar2=None, op0=ALU.add))
            sop(lambda e: e.activation(out=p_vt1, in_=p_var, func=AF.Ln),
                wait=[ev_v])
            ev_rstd = sop(lambda e: e.activation(out=p_rstdr, in_=p_vt1, func=AF.Exp,
                                                 scale=-0.5))
            top(lambda e: e.matmul(psD[:, 0:TB], onesf[0:1, :], p_mrow[0:1, :],
                                   start=True, stop=True), wait=[em.now("s_v")])
            top(lambda e: e.matmul(psC[:, 0:TB], onesf[0:1, :], p_rstdr[0:1, :],
                                   start=True, stop=True), wait=[ev_rstd])
            ev_bcast = em.now("s_pe")
            evs = []
            for i in range(8):
                vop(lambda e, i=i: e.tensor_tensor(out=g_tmp[:], in0=src_ap[:, i, :],
                                                   in1=psD[:, 0:TB], op=ALU.subtract),
                    wait=[ev_bcast])
                evs.append(vop(lambda e, i=i: e.tensor_tensor(
                    out=out_ap[:, i, :], in0=g_tmp[:], in1=psC[:, 0:TB],
                    op=ALU.mult)))
            return evs

        ev_x2_tiles = layer_norm(g_s, g_x2b, ev_s_tiles)
        ev_x2 = em.now("s_v")
        if blk + 1 < NB:
            prev_dl_rd = [em.now("qdl0"), em.now("qdl1")]
            prev_rr_rd = [em.now("qrr0"), em.now("qrr1")]
            prev_er_rd = [em.now("qer0"), em.now("qer1")]
            ch_pending = p0_chain(blk + 1)

        if dbg and blk == 4:
            for i in range(8):
                vop(lambda e, i=i: e.tensor_copy(g_tmp[:], g_s[:, i, :]),
                    wait=[ev_x2, em.now("qdbg")])
                dsy("qdbg", lambda e, i=i: e.dma_start(out=dbg_t["d_s"][:, i * TB:(i + 1) * TB],
                                               in_=g_tmp[:]), wait=[em.now("s_v")])
            for i in range(8):
                vop(lambda e, i=i: e.tensor_copy(g_tmp[:], g_x2b[:, i, :]),
                    wait=[ev_x2, em.now("qdbg")])
                dsy("qdbg", lambda e, i=i: e.dma_start(out=dbg_t["d_x2"][:, i * TB:(i + 1) * TB],
                                               in_=g_tmp[:]), wait=[em.now("s_v")])
            dgp("qdbg2", lambda e: e.dma_start(out=dbg_t["d_yg"][:], in_=yg_f), wait=[ev_x2])
            ev_dbgdone = em.now("qdbg")

        for kc in range(8):
            top(lambda e, kc=kc: e.matmul(psM[0:16, 0:TB], p_vb[:, kc, :],
                                          g_x2b[:, kc, :], start=(kc == 0),
                                          stop=(kc == 7)),
                wait=[ev_x2_tiles[kc], ev_smgp, ev_ecols, ev_dtwT])
        ev_s1m = em.now("s_pe")
        ev_s1 = vop(lambda e: e.tensor_copy(g_s1[:], psM[0:16, 0:TB]), wait=[ev_s1m])
        top(lambda e: e.matmul(psM[0:16, 0:TB], p_fbt[:], g_s1[:], start=True,
                               stop=True), wait=[ev_s1])
        ev_s2m = em.now("s_pe")
        ev_s2 = vop(lambda e: e.tensor_scalar(out=g_s2[:], in0=psM[0:16, 0:TB],
                                              scalar1=p_gatec, scalar2=None,
                                              op0=ALU.mult), wait=[ev_s2m])
        prev_s2 = ev_s2

        def r2slot(hi):
            if hi < 16:
                return yg_f.bitcast(BF16)[:, hi * TB:(hi + 1) * TB]
            return g_r2a_hi[:, hi - 16, :]
        ev_s2_tiles = [None] * 8
        ev_o_done = [None, None]
        ev_r2_list = []
        for hi in range(32):
            pfc = psA if hi % 2 == 0 else psB
            wfc_w = [ev_x2, ev_uts, ev_s2]
            if len(ev_r2_list) >= 2:
                wfc_w.append(ev_r2_list[-2])
            for kc in range(8):
                top(lambda e, hi=hi, kc=kc, pfc=pfc: e.matmul(
                    pfc[:, 0:TB], fc8[:, kc, hi * 128:(hi + 1) * 128],
                    g_x2b[:, kc, :], start=(kc == 0), stop=False), wait=wfc_w)
            top(lambda e, hi=hi, pfc=pfc: e.matmul(
                pfc[:, 0:TB], uts[:, hi * 128:(hi + 1) * 128],
                g_s2[:], start=False, stop=True), wait=wfc_w)
            ev_hmm = em.now("s_pe")
            ev_rs = sop(lambda e, hi=hi, pfc=pfc: e.activation(
                out=g_rs[hi % 2][:], in_=pfc[:, 0:TB], func=AF.Relu,
                scale=p_scale[:, hi:hi + 1]),
                wait=[ev_hmm] + ([ev_xf] if hi < 2 else []))
            ev_r2_list.append(sop(lambda e, hi=hi: e.activation(
                out=r2slot(hi), in_=g_rs[hi % 2][:], func=AF.Square),
                wait=[ev_rs]))
        last_r2 = ev_r2_list[-1]
        prev_f1rs = em.now("s_sc")
        for dt_ in range(8):
            bank = psD if dt_ % 2 == 0 else psC
            w_pj = [last_r2]
            if ev_o_done[dt_ % 2] is not None:
                w_pj.append(ev_o_done[dt_ % 2])
            for hi in range(32):
                top(lambda e, hi=hi, dt_=dt_, bank=bank: e.matmul(
                    bank[:, 0:TB],
                    pj8[:, hi, dt_ * 128:(dt_ + 1) * 128], r2slot(hi),
                    start=(hi == 0), stop=(hi == 31)), wait=w_pj)
            ev_pjmm = em.now("s_pe")
            ev_o = vop(lambda e, dt_=dt_, bank=bank: e.scalar_tensor_tensor(
                out=g_s[:, dt_, :], in0=bank[:, 0:TB],
                scalar=p_scale2[:, dt_:dt_ + 1], in1=g_x2b[:, dt_, :],
                op0=ALU.mult, op1=ALU.add), wait=[ev_pjmm])
            ev_o_done[dt_ % 2] = ev_o
            ev_s2_tiles[dt_] = ev_o

        ev_out_tiles = layer_norm(g_s, g_xf, ev_s2_tiles)
        prev_store = dgp("qst", lambda e, blk=blk: e.dma_start(
            out=yfm.rearrange("(a p) t -> p a t", p=128)[:, :, (blk - 4) * TB:(blk - 3) * TB],
            in_=g_xf[:, :, :]), wait=ev_out_tiles)
        prev_p3_v = em.now("s_v")
        prev_p3_pe = em.now("s_pe")

    all_dma = ["qsm", "qsg", "qwt0", "qwt1", "qwt2", "qwt3", "qsc", "qrp",
               "qut", "wr0", "wr1", "wr2",
               "wo0", "wo1", "qx", "qxf", "qdl0", "qdl1", "qrr0", "qrr1",
               "qer0", "qer1", "qw0", "qw2", "qw3", "qde", "qerow", "qtc", "qdt",
               "qst", "qdbg", "qdbg2", "qon"]
    em.op("sync", lambda e: e.nop(), wait=[em.now(q) for q in all_dma])

    with nc.Block() as block:
        em.emit(block)
    return nc, dbg_t


_prog_cache = {}


def _get_prog(dbg=False):
    if dbg not in _prog_cache:
        _prog_cache[dbg] = build_program(dbg=dbg)
    return _prog_cache[dbg]


def _prep_weights(i):
    W_in = np.asarray(i["L0_attn_W_in"], np.float32)
    W_in_pad = np.concatenate([W_in, np.zeros((4352 - 4240, 1024), np.float32)], 0)
    win = np.ascontiguousarray(W_in_pad.reshape(34, 128, 8, 128).transpose(0, 3, 2, 1))
    W_out = np.asarray(i["L0_attn_W_out"], np.float32)
    wout = np.ascontiguousarray(W_out.reshape(8, 128, 16, 128).transpose(0, 3, 2, 1))
    conv_w = np.asarray(i["L0_attn_conv_w"], np.float32).reshape(NTIL_X, 128, DCONV)
    conv_b = np.ascontiguousarray(np.asarray(i["L0_attn_conv_b"], np.float32).reshape(NTIL_X, 128).T)
    return {
        "win_t": win,
        "wout_t": wout,
        "wfc": np.ascontiguousarray(np.asarray(i["L0_ffn_w_fc"], np.float32)),
        "wpj": np.ascontiguousarray(np.asarray(i["L0_ffn_w_proj"], np.float32)),
        "cvw": np.ascontiguousarray(conv_w),
        "cvb": np.ascontiguousarray(conv_b),
        "alog": np.ascontiguousarray(np.asarray(i["L0_attn_A_log"], np.float32).reshape(16, 1)),
        "dtbias": np.ascontiguousarray(np.asarray(i["L0_attn_dt_bias"], np.float32).reshape(16, 1)),
        "dskip": np.ascontiguousarray(np.asarray(i["L0_attn_D_skip"], np.float32).reshape(1, 16)),
        "ffnu": np.ascontiguousarray(np.asarray(i["L0_ffn_u"], np.float32).T),
        "ffnv": np.ascontiguousarray(
            np.asarray(i["L0_ffn_v"], np.float32).T.reshape(8, 128, 16).transpose(1, 0, 2)),
        "fbt": np.ascontiguousarray(np.asarray(i["L0_ffn_fast_b"], np.float32).T),
        "gatei": np.asarray(i["L0_ffn_gate"], np.float32).reshape(1, 1),
    }


def kernel(**inputs):
    nc, _ = _get_prog(dbg=False)
    wts = _prep_weights(inputs)
    x = np.asarray(inputs["L0_attn_x"], np.float32)
    in_maps = []
    for core in range(8):
        b, hf = core // 2, core % 2
        own = x[b, hf * T_OWN:(hf + 1) * T_OWN]
        ctx = x[b, 0:T_OWN] if hf == 1 else np.zeros((T_OWN, D_MODEL), np.float32)
        m = dict(wts)
        m["xfm"] = np.ascontiguousarray(np.concatenate([ctx, own], 0).T)
        m["flagi"] = np.asarray([[1.0 if hf == 1 else 0.0]], np.float32)
        in_maps.append(m)
    res = run_bass_kernel_spmd(nc, in_maps, core_ids=list(range(8)))
    out = np.zeros((4, 4096, 1024), np.float32)
    for core in range(8):
        b, hf = core // 2, core % 2
        out[b, hf * T_OWN:(hf + 1) * T_OWN] = res.results[core]["yfm"].T
    return out


# version 18
# speedup vs baseline: 1.1801x; 1.1801x over previous
import numpy as np
import concourse.bass as bass
import concourse.mybir as mybir
from concourse.bass_utils import run_bass_kernel_spmd

F32 = mybir.dt.float32
F32R = mybir.dt.float32r
BF16 = mybir.dt.bfloat16
FP8 = mybir.dt.float8e4
U8 = mybir.dt.uint8
AF = mybir.ActivationFunctionType
ALU = mybir.AluOpType
AX = mybir.AxisListType.X

D_MODEL = 1024
NH = 16
HD = 128
DCONV = 4
NTIL_X = 17
HID = 4096
TB = 512
CH = 128
NC = 4
NB = 8
T_OWN = 2048
EPS = 1e-5
ENGS = ("tensor", "vector", "scalar", "sync", "gpsimd")


class Em:
    def __init__(self, nc):
        self.nc = nc
        self.items = []
        self.cnt = {}
        self.sems = {}

    def sem(self, name):
        h = self.nc.semaphore(name).__enter__()
        self.sems[name] = h
        self.cnt[name] = 0
        return name

    SELF_CHAIN = {"vector": "s_v", "scalar": "s_sc", "gpsimd": "s_gq"}

    def op(self, eng, fn, wait=(), inc=()):
        ws = []
        sc_sem = self.SELF_CHAIN.get(eng)
        if sc_sem is not None and self.cnt.get(sc_sem, 0) > 0:
            ws.append((sc_sem, self.cnt[sc_sem]))
        for w in wait:
            if w is None:
                continue
            s, v = w
            assert v <= self.cnt[s], f"wait {s}>={v} but only {self.cnt[s]} recorded"
            if v > 0:
                ws.append((s, v))
        for (s, n) in inc:
            self.cnt[s] += n
        self.items.append((eng, fn, ws, list(inc)))
        return None if not inc else (inc[0][0], self.cnt[inc[0][0]])

    def now(self, s):
        return (s, self.cnt[s])

    def emit(self, block):
        def mk(eng_name):
            def emit_fn(e):
                water = {}
                for eng, fn, ws, incs in self.items:
                    if eng != eng_name:
                        continue
                    for (s, v) in ws:
                        if water.get(s, 0) >= v:
                            continue
                        e.wait_ge(self.sems[s], v)
                        water[s] = v
                    ins = fn(e)
                    for (s, n) in incs:
                        ins.then_inc(self.sems[s], n)
            return emit_fn
        for name in ENGS:
            getattr(block, name)(mk(name))


class Pool2D:
    def __init__(self, nc, name, nbytes):
        self.t = nc.sbuf_tensor(name, [128, nbytes], U8).__enter__()
        self.nbytes = nbytes
        self.rects = []

    def rect(self, p0, np_, b0, nb, dtype, shape, tag=""):
        assert b0 + nb <= self.nbytes and p0 + np_ <= 128, (tag, b0 + nb, self.nbytes)
        for (q0, qn, c0, cn, qtag) in self.rects:
            if not (p0 + np_ <= q0 or q0 + qn <= p0 or b0 + nb <= c0 or c0 + cn <= b0):
                raise RuntimeError(f"pool overlap {tag} vs {qtag}")
        self.rects.append((p0, np_, b0, nb, tag))
        ap = self.t[p0:p0 + np_, b0:b0 + nb].bitcast(dtype)
        if len(shape) > 2:
            pat = "p (" + " ".join(chr(97 + i) for i in range(len(shape) - 1)) + ") -> p " + \
                  " ".join(chr(97 + i) for i in range(len(shape) - 1))
            kw = {chr(97 + i): shape[i + 1] for i in range(len(shape) - 2)}
            ap = ap.rearrange(pat, **kw)
        return ap

    def free_rect(self, p0, np_, b0, nb, dtype, shape, tag=""):
        assert b0 + nb <= self.nbytes and p0 + np_ <= 128
        ap = self.t[p0:p0 + np_, b0:b0 + nb].bitcast(dtype)
        if len(shape) > 2:
            pat = "p (" + " ".join(chr(97 + i) for i in range(len(shape) - 1)) + ") -> p " + \
                  " ".join(chr(97 + i) for i in range(len(shape) - 1))
            kw = {chr(97 + i): shape[i + 1] for i in range(len(shape) - 2)}
            ap = ap.rearrange(pat, **kw)
        return ap


def build_program(dbg=False):
    nc = bass.Bass("TRN2", target_bir_lowering=False, debug=False)

    xfm = nc.dram_tensor("xfm", [D_MODEL, 2 * T_OWN], F32, kind="ExternalInput").ap()
    win_t = nc.dram_tensor("win_t", [34, 128, 8, 128], F32, kind="ExternalInput").ap()
    wout_t = nc.dram_tensor("wout_t", [8, 128, 16, 128], F32, kind="ExternalInput").ap()
    wfc = nc.dram_tensor("wfc", [HID, D_MODEL], F32, kind="ExternalInput").ap()
    wpj = nc.dram_tensor("wpj", [D_MODEL, HID], F32, kind="ExternalInput").ap()
    cvw = nc.dram_tensor("cvw", [NTIL_X, 128, DCONV], F32, kind="ExternalInput").ap()
    cvb = nc.dram_tensor("cvb", [128, NTIL_X], F32, kind="ExternalInput").ap()
    alog = nc.dram_tensor("alog", [16, 1], F32, kind="ExternalInput").ap()
    dtbias = nc.dram_tensor("dtbias", [16, 1], F32, kind="ExternalInput").ap()
    dskip = nc.dram_tensor("dskip", [1, 16], F32, kind="ExternalInput").ap()
    ffnu = nc.dram_tensor("ffnu", [16, HID], F32, kind="ExternalInput").ap()
    ffnv = nc.dram_tensor("ffnv", [128, 8, 16], F32, kind="ExternalInput").ap()
    fbt = nc.dram_tensor("fbt", [16, 16], F32, kind="ExternalInput").ap()
    gatei = nc.dram_tensor("gatei", [1, 1], F32, kind="ExternalInput").ap()
    flagi = nc.dram_tensor("flagi", [1, 1], F32, kind="ExternalInput").ap()
    yfm = nc.dram_tensor("yfm", [D_MODEL, T_OWN], F32, kind="ExternalOutput").ap()
    drows = nc.dram_tensor("drows", [4, 16, TB], F32, kind="Internal").ap()
    drows3b = nc.dram_tensor("drows3b", [16, TB], BF16, kind="Internal").ap()
    win_s = nc.dram_tensor("win_s", [34, 128, 8, 128], BF16, kind="Internal").ap()
    wout_s = nc.dram_tensor("wout_s", [8, 128, 16, 128], BF16, kind="Internal").ap()
    x_s = nc.dram_tensor("x_s", [128, 8, 2 * T_OWN], BF16, kind="Internal").ap()
    de = nc.dram_tensor("de", [16, NC], F32, kind="Internal").ap()
    dt_all = nc.dram_tensor("dt_all", [16, 8 * TB], F32, kind="Internal").ap()
    dsc = nc.dram_tensor("dsc", [32, 128], F32, kind="Internal").ap()
    dbg_t = {}
    if dbg:
        for nm, sh in [("d_dt", [16, TB]), ("d_cs", [16, TB]), ("d_xssm", [128, TB]),
                       ("d_scores", [128, NC * 128]), ("d_yg", [128, NH * TB]),
                       ("d_s", [128, 8 * TB]), ("d_x2", [128, 8 * TB]),
                       ("d_r", [64, NH * 128]), ("d_fc8", [128, 1024]),
                       ("d_pj8", [128, 32 * 1024]),
                       ("d_scale", [128, 32])]:
            dbg_t[nm] = nc.dram_tensor(nm, sh, F32, kind="ExternalOutput").ap()

    def sb(name, shape, dt):
        return nc.sbuf_tensor(name, shape, dt).__enter__()

    def psm(name, shape, dt):
        return nc.psum_tensor(name, shape, dt).__enter__()

    fc8 = sb("fc8", [128, 8 * HID], FP8).ap().rearrange("p (k c) -> p k c", k=8)
    pj8 = sb("pj8", [128, 32 * D_MODEL], FP8).ap().rearrange("p (k c) -> p k c", k=32)
    cdiag = sb("cdiag", [128, NTIL_X * DCONV * 128], BF16).ap().rearrange(
        "p (t j c) -> p t j c", t=NTIL_X, j=DCONV)
    disk = sb("disk", [128, NH * 128], BF16).ap().rearrange("p (h c) -> p h c", h=NH)
    uts = sb("uts", [16, HID], BF16).ap()
    yg_f = sb("yg", [128, NH * TB], BF16).ap()
    yg = yg_f.rearrange("p (h t) -> p h t", h=NH)
    xob = sb("xob", [128, 8 * TB], BF16).ap().rearrange("p (a t) -> p a t", a=8)
    rst = sb("rst", [64, NH * 128], F32).ap()
    wring = sb("wring", [128, 3 * 8 * 128], BF16).ap().rearrange(
        "p (s a c) -> p s a c", s=3, a=8)
    woutr = sb("woutr", [128, 2 * 16 * 128], BF16).ap().rearrange(
        "p (s a c) -> p s a c", s=2, a=16)
    onesf = sb("onesf", [128, 128], F32).ap()
    onesb = sb("onesb", [128, 128], BF16).ap()
    idf = sb("idf", [128, 128], F32).ap()
    idb = sb("idb", [128, 128], BF16).ap()
    maskT = sb("maskT", [128, 128], BF16).ap()
    bcs = sb("bcs", [128, TB], BF16).ap()
    tcc = sb("tcc", [64, TB], BF16).ap()
    halo = sb("halo", [128, NTIL_X * 4], BF16).ap().rearrange("p (t k) -> p t k", t=NTIL_X)

    P = Pool2D(nc, "pool", 21272)
    p_dt = P.rect(64, 16, 0, 2048, F32, [16, TB], "dt")
    p_wdec = P.rect(64, 16, 2048, 2048, F32, [16, TB], "wdec")
    p_cs = P.rect(64, 16, 4096, 2048, F32, [16, TB], "cs")
    p_E = P.rect(64, 16, 6144, 2048, F32, [16, TB], "E")
    p_ncs = P.rect(64, 16, 8192, 2048, F32, [16, TB], "ncs")
    p_t1 = P.rect(64, 16, 10240, 2048, F32, [16, TB], "t1")
    p_gate = P.rect(64, 16, 12288, 2048, F32, [16, TB], "gate")
    p_dl = [P.rect(0, 2, 2048 + i * 2048, 2048, F32, [2, TB], f"dl{i}") for i in range(2)]
    p_rr = [P.rect(0, 2, 6144 + i * 2048, 2048, F32, [2, TB], f"rr{i}") for i in range(2)]
    p_er = [P.rect(64, 1, 14336 + i * 1024, 1024, BF16, [1, TB], f"er{i}") for i in range(2)]
    p_erow = P.rect(64, 1, 16384, 256, F32, [1, 64], "erow")
    p_convw = P.rect(0, 128, 16640, 272, F32, [128, NTIL_X, DCONV], "convw")
    p_convb = P.rect(0, 128, 16912, 68, F32, [128, NTIL_X], "convb")
    p_acol = P.rect(64, 16, 16980, 4, F32, [16, 1], "acol")
    p_dtb = P.rect(64, 16, 16984, 4, F32, [16, 1], "dtbias")
    p_gatec = P.rect(0, 16, 16988, 4, F32, [16, 1], "gatecol")
    p_flagc = P.rect(0, 64, 16992, 4, F32, [64, 1], "flagcol")
    p_dtwT = P.rect(0, 128, 16996, 512, F32, [128, NC, 32], "dtwT")
    p_ecols = P.rect(0, 64, 17508, 256, F32, [64, NC, 16], "ecols")
    p_bt = P.rect(0, 128, 17764, 512, BF16, [128, NC, 64], "bt")
    p_cbm = P.rect(0, 128, 18276, 1024, BF16, [128, NC, 128], "cbm")
    p_scale = P.rect(0, 128, 19300, 128, F32, [128, 32], "scale")
    p_recip = P.rect(0, 128, 19428, 128, F32, [128, 32], "recip")
    p_scale2 = P.rect(0, 128, 19556, 32, F32, [128, 8], "scale2")
    p_vb = P.rect(0, 128, 19588, 256, BF16, [128, 8, 16], "vb")
    p_fbt = P.rect(0, 16, 19844, 32, BF16, [16, 16], "fbt")
    p_Eb = P.rect(64, 16, 20224, 1024, BF16, [16, TB], "Eb")
    p_sums = P.rect(0, 128, 21248, 16, F32, [128, 4], "sums")
    p_thr = P.rect(0, 128, 19876, 128, F32, [128, 32], "thr")
    p_nthr = P.rect(0, 128, 20004, 128, F32, [128, 32], "nthr")
    p_thr2 = P.rect(0, 128, 20132, 32, F32, [128, 8], "thr2")
    p_nthr2 = P.rect(0, 128, 20164, 32, F32, [128, 8], "nthr2")
    q_r32 = P.rect(0, 32, 20224, 512, F32, [32, 128], "qr32")
    p_magic = P.rect(0, 128, 21264, 4, F32, [128, 1], "magic")
    p_nmagic = P.rect(0, 128, 21268, 4, F32, [128, 1], "nmagic")

    AR = Pool2D(nc, "arena", 45056)
    a_xbc = [AR.rect(0, 128, i * 2080, 2080, BF16, [128, 1040], f"xbc{i}") for i in range(2)]
    a_xssm = [AR.rect(0, 128, 4160 + i * 1024, 1024, BF16, [128, TB], f"xssm{i}") for i in range(2)]
    a_zsil = [AR.rect(0, 128, 6208 + i * 1024, 1024, BF16, [128, TB], f"zsil{i}") for i in range(2)]
    a_xT = [AR.rect(0, 128, 8256 + i * 1024, 1024, BF16, [128, NC, 128], f"xT{i}") for i in range(2)]
    a_dm = AR.rect(0, 128, 10304, 2048, F32, [128, NC * 128], "dm")
    a_sexp = [AR.rect(0, 128, 12352 + i * 1024, 1024, BF16, [128, NC, 128], f"sexp{i}") for i in range(2)]
    a_m = [AR.rect(0, 128, 14400 + i * 1024, 1024, BF16, [128, NC, 128], f"m{i}") for i in range(2)]
    a_scores = AR.rect(0, 128, 16448, NH * 1024, BF16, [128, NH, NC, 128], "scores")
    a_bdec = [AR.rect(0, 128, 32832 + i * 512, 512, BF16, [128, NC, 64], f"bdec{i}") for i in range(2)]
    a_cexp = [AR.rect(0, 64, 33856 + i * 2048, 2048, F32, [64, NC, 128], f"cexp{i}") for i in range(2)]
    g_s = AR.free_rect(0, 128, 0, 16384, F32, [128, 8, TB], "g_s")
    g_sq = AR.free_rect(0, 128, 16384, 1024, BF16, [128, TB], "g_sq")
    g_tmp = AR.free_rect(0, 128, 18432, 2048, F32, [128, TB], "g_tmp")
    g_x2b = AR.free_rect(0, 128, 20480, 8192, BF16, [128, 8, TB], "g_x2b")
    g_xf = AR.free_rect(0, 128, 28672, 16384, F32, [128, 8, TB], "g_xf")
    g_rs = [AR.free_rect(0, 128, 16384 + i * 1024, 1024, BF16, [128, TB], f"g_rs{i}") for i in range(2)]
    g_r2a_hi = AR.free_rect(0, 128, 28672, 16384, BF16, [128, 16, TB], "g_r2hi")
    g_s1 = AR.free_rect(0, 16, 19456, 1024, BF16, [16, TB], "g_s1")
    g_s2 = AR.free_rect(0, 16, 18432, 1024, BF16, [16, TB], "g_s2")
    p_mrow = AR.free_rect(0, 1, 28672, 2048, F32, [1, TB], "mrow")
    p_rstdr = AR.free_rect(0, 1, 30720, 2048, F32, [1, TB], "rstdrow")
    p_var = AR.free_rect(0, 1, 32768, 2048, F32, [1, TB], "var")
    p_vt1 = AR.free_rect(0, 1, 34816, 2048, F32, [1, TB], "vt1")
    q_ut = AR.free_rect(0, 16, 28672, 16384, F32, [16, HID], "q_ut")
    q_rrow = AR.free_rect(0, 1, 0, 8192, BF16, [1, HID], "q_rrow")

    psA = psm("psA", [128, 512], F32).ap()
    psB = psm("psB", [128, 512], F32).ap()
    psD = psm("psD", [128, 512], F32).ap()
    psC = psm("psC", [128, 512], F32).ap()
    psY = psm("psY", [128, 512], F32).ap()
    psS = psm("psS", [128, 512], F32).ap()
    psT = psm("psT", [128, 1024], BF16).ap()
    psM = psm("psM", [128, 512], F32).ap()

    em = Em(nc)
    for sname in ("s_pe", "s_v", "s_sc", "s_gq", "qsm", "qsg",
                  "qwt0", "qwt1", "qwt2", "qwt3", "qsc",
                  "qrp", "qut", "wr0", "wr1", "wr2", "wo0", "wo1", "qx", "qxf",
                  "qdl0", "qdl1", "qrr0", "qrr1", "qer0", "qer1", "qw0", "qw2",
                  "qw3", "qde", "qerow", "qtc", "qst", "qdbg", "qdbg2", "qon",
                  "qdt"):
        em.sem(sname)

    def vop(fn, wait=()):
        return em.op("vector", fn, wait=wait, inc=[("s_v", 1)])

    def sop(fn, wait=()):
        return em.op("scalar", fn, wait=wait, inc=[("s_sc", 1)])

    def top(fn, wait=()):
        return em.op("tensor", fn, wait=wait, inc=[("s_pe", 1)])

    def gop(fn, wait=()):
        return em.op("gpsimd", fn, wait=wait, inc=[("s_gq", 1)])

    def dsy(sem, fn, wait=()):
        return em.op("sync", fn, wait=wait, inc=[(sem, 16)])

    def dgp(sem, fn, wait=()):
        return em.op("gpsimd", fn, wait=wait, inc=[(sem, 16)])

    ev_xs = dgp("qsc", lambda e: e.dma_start(
        out=x_s, in_=xfm.rearrange("(a p) t -> p a t", p=128)))
    for i in range(4):
        n0, n1 = i * 9, min((i + 1) * 9, 34)
        dgp("qsc", lambda e, n0=n0, n1=n1: e.dma_start(out=win_s[n0:n1],
                                                       in_=win_t[n0:n1]))
    dgp("qsc", lambda e: e.dma_start(out=wout_s, in_=wout_t))
    ev_casts = em.now("qsc")
    gop(lambda e: e.memset(onesf[:], 1.0))
    gop(lambda e: e.memset(idf[:], 0.0))
    gop(lambda e: e.affine_select(out=idf[:], in_=idf[:], compare_op=ALU.not_equal,
                                  fill=1.0, base=0, pattern=[[-1, 128]], channel_multiplier=1))
    gop(lambda e: e.memset(idb[:], 0.0))
    gop(lambda e: e.affine_select(out=idb[:], in_=idb[:], compare_op=ALU.not_equal,
                                  fill=1.0, base=0, pattern=[[-1, 128]], channel_multiplier=1))
    gop(lambda e: e.memset(maskT[:], 1.0))
    gop(lambda e: e.affine_select(out=maskT[:], in_=maskT[:], compare_op=ALU.is_ge,
                                  fill=0.0, base=0, pattern=[[1, 128]], channel_multiplier=-1))
    gop(lambda e: e.memset(p_gate, 1.0))
    gop(lambda e: e.affine_select(out=p_gate.rearrange("p (c t) -> p c t", c=NC),
                                  in_=p_gate.rearrange("p (c t) -> p c t", c=NC),
                                  compare_op=ALU.is_gt, fill=0.0, base=0,
                                  pattern=[[0, NC], [1, CH]], channel_multiplier=0))
    gop(lambda e: e.memset(p_t1, 1.0))
    gop(lambda e: e.memset(halo[:, :, :], 0.0))
    gop(lambda e: e.memset(rst[:], 0.0))
    gop(lambda e: e.memset(p_magic, MAGIC14))
    gop(lambda e: e.memset(p_nmagic, -MAGIC14))
    ev_consts = em.now("s_gq")
    ev_magic = ev_consts
    ev_r0 = ev_consts
    ev_ones = vop(lambda e: e.tensor_copy(onesb[:], onesf[:]), wait=[ev_consts])

    dsy("qsm", lambda e: e.dma_start(out=p_convw, in_=cvw.rearrange("t p k -> p t k")))
    dsy("qsm", lambda e: e.dma_start(out=p_convb, in_=cvb[:]))
    dsy("qsm", lambda e: e.dma_start(out=p_acol, in_=alog[:]))
    dsy("qsm", lambda e: e.dma_start(out=p_dtb, in_=dtbias[:]))
    dsy("qsm", lambda e: e.dma_start(out=p_dl[0][0:1, 0:1], in_=gatei[:]))
    dsy("qsm", lambda e: e.dma_start(out=p_dl[0][0:1, 1:2], in_=flagi[:]))
    dsy("qsm", lambda e: e.dma_start(out=p_dl[0][0:1, 2:18], in_=dskip[:]))
    ev_small = em.now("qsm")
    dgp("qsg", lambda e: e.dma_start(out=p_vb, in_=ffnv[:]), wait=[ev_consts])
    dgp("qsg", lambda e: e.dma_start(out=p_fbt, in_=fbt[:]))
    ev_smgp = em.now("qsg")
    dsy("qon", lambda e: e.dma_start(out=drows[1], in_=p_t1[:, 0:TB]), wait=[ev_consts])

    sop(lambda e: e.activation(out=p_dt[:, 0:1], in_=p_acol, func=AF.Exp),
        wait=[ev_small])
    ev_acol = vop(lambda e: e.tensor_scalar(out=p_acol, in0=p_dt[:, 0:1], scalar1=-1.0,
                                            scalar2=None, op0=ALU.mult),
                  wait=[em.now("s_sc")])
    top(lambda e: e.matmul(psM[0:16, 0:1], onesf[0:1, 0:16], p_dl[0][0:1, 0:1],
                           start=True, stop=True), wait=[ev_small, ev_consts])
    top(lambda e: e.matmul(psM[0:64, 1:2], onesf[0:1, 0:64], p_dl[0][0:1, 1:2],
                           start=True, stop=True))
    top(lambda e: e.matmul(psM[0:128, 2:18], onesf[0:1, 0:128], p_dl[0][0:1, 2:18],
                           start=True, stop=True))
    ev_bc = em.now("s_pe")
    vop(lambda e: e.tensor_copy(p_gatec, psM[0:16, 0:1]), wait=[ev_bc])
    vop(lambda e: e.tensor_copy(p_flagc, psM[0:64, 1:2]))
    for h in range(NH):
        vop(lambda e, h=h: e.tensor_scalar(out=disk[:, h, :], in0=idb[:],
                                           scalar1=psM[0:128, 2 + h:3 + h], scalar2=None,
                                           op0=ALU.mult), wait=[ev_bc, ev_consts])
    ev_disk = em.now("s_v")
    for til in range(NTIL_X):
        for tap in range(DCONV):
            vop(lambda e, til=til, tap=tap: e.tensor_scalar(
                out=cdiag[:, til, tap, :], in0=idb[:],
                scalar1=p_convw[:, til, tap:tap + 1], scalar2=None, op0=ALU.mult),
                wait=[ev_small, ev_consts])
    ev_cdiag = em.now("s_v")

    MAGIC14 = float(np.float32(1.5 * 2 ** 9))
    t_ring = [AR.free_rect(0, 128, 16448 + i * 4096, 4096, F32, [128, 1024],
                           f"tring{i}") for i in range(3)]
    t_hq = [AR.free_rect(0, 128, b, 4096, F32, [128, 1024], f"t_hq{i}")
            for i, b in enumerate((33856, 28736))]
    t_l = [AR.free_rect(0, 128, b, 4096, F32, [128, 1024], f"t_l{i}")
           for i, b in enumerate((37952, 10304))]
    t_out = [AR.free_rect(0, 128, b, 2048, BF16, [128, 1024], f"t_out{i}")
             for i, b in enumerate((14400, 42048))]

    tern_srcs = ([wfc[nt * 128:(nt + 1) * 128, :] for nt in range(32)] +
                 [wpj[nt * 128:(nt + 1) * 128, cc * 1024:(cc + 1) * 1024]
                  for nt in range(8) for cc in range(4)])
    tern_srcs = tern_srcs + tern_srcs
    ring_ev = [None] * 3
    ring_busy = [None] * 3
    ld_next = [0]

    def ensure_loads(upto):
        while ld_next[0] <= min(upto, len(tern_srcs) - 1):
            li = ld_next[0]
            s = li % 3
            ring_ev[s] = dgp(f"qwt{s}", lambda e, s=s, src=tern_srcs[li]: e.dma_start(
                out=t_ring[s][:], in_=src), wait=[ring_busy[s]])
            ld_next[0] += 1

    rd_hq = [None, None]
    rd_l = [None, None]
    drip_q = []
    drip_pending = [None]

    def scale_job(li, cc, nfin, ncols, scol, tcol, ncol):
        def emit():
            ensure_loads(li + 2)
            s = li % 3
            p = li % 2
            ev_abs = sop(lambda e, s=s, p=p: e.activation(out=t_hq[p][:],
                                                          in_=t_ring[s][:],
                                                          func=AF.Abs),
                         wait=[ring_ev[s], ev_magic, rd_hq[p]])
            ring_busy[s] = ev_abs
            sop(lambda e, p=p: e.activation(out=t_l[p][:], in_=t_hq[p][:],
                                            func=AF.Identity, bias=p_magic),
                wait=[rd_l[p]])
            ev_m2 = sop(lambda e, p=p: e.activation(out=t_l[p][:], in_=t_l[p][:],
                                                    func=AF.Identity,
                                                    bias=p_nmagic))
            ev_sub = gop(lambda e, p=p: e.tensor_tensor(
                out=t_hq[p][:], in0=t_hq[p][:], in1=t_l[p][:],
                op=ALU.subtract), wait=[ev_m2])

            def red_part():
                c0, c1 = (0, 1) if nfin == 1 else (2, 3)
                rd_l[p] = vop(lambda e, c0=c0, p=p: e.tensor_reduce(
                    p_sums[:, c0:c0 + 1], t_l[p][:], axis=AX, op=ALU.add),
                    wait=[ev_sub])
                rd_hq[p] = vop(lambda e, c1=c1, p=p: e.tensor_reduce(
                    p_sums[:, c1:c1 + 1], t_hq[p][:], axis=AX, op=ALU.add))
                if nfin > 1:
                    if cc == 0:
                        vop(lambda e: e.tensor_copy(p_sums[:, 0:1], p_sums[:, 2:3]))
                        vop(lambda e: e.tensor_copy(p_sums[:, 1:2], p_sums[:, 3:4]))
                    else:
                        vop(lambda e: e.tensor_tensor(out=p_sums[:, 0:1],
                                                      in0=p_sums[:, 0:1],
                                                      in1=p_sums[:, 2:3], op=ALU.add))
                        vop(lambda e: e.tensor_tensor(out=p_sums[:, 1:2],
                                                      in0=p_sums[:, 1:2],
                                                      in1=p_sums[:, 3:4], op=ALU.add))
                if cc == nfin - 1:
                    vop(lambda e: e.tensor_tensor(out=scol, in0=p_sums[:, 0:1],
                                                  in1=p_sums[:, 1:2], op=ALU.add))
                    vop(lambda e: e.tensor_scalar(out=scol, in0=scol,
                                                  scalar1=1.0 / ncols, scalar2=1e-5,
                                                  op0=ALU.mult, op1=ALU.max))
                    vop(lambda e: e.tensor_scalar(out=tcol, in0=scol, scalar1=0.5,
                                                  scalar2=None, op0=ALU.mult))
                    vop(lambda e: e.tensor_scalar(out=ncol, in0=tcol, scalar1=-1.0,
                                                  scalar2=None, op0=ALU.mult))
            return red_part
        return emit

    tern_slot = [0]
    tern_evac_slot = [None, None]

    def cmp_job(li, tcol, ncol, dst_fn):
        def emit():
            ensure_loads(li + 2)
            s = li % 3
            p = li % 2
            sl = tern_slot[0] % 2
            tern_slot[0] += 1
            sop(lambda e, s=s, p=p: e.activation(out=t_hq[p][:], in_=t_ring[s][:],
                                                 func=AF.Sign, bias=ncol),
                wait=[ring_ev[s], rd_hq[p]])
            ev_s2 = sop(lambda e, s=s, p=p: e.activation(out=t_l[p][:],
                                                         in_=t_ring[s][:],
                                                         func=AF.Sign, bias=tcol),
                        wait=[rd_l[p]])
            ring_busy[s] = ev_s2
            ev_o = gop(lambda e, sl=sl, p=p: e.tensor_tensor(
                out=t_out[sl][:], in0=t_hq[p][:], in1=t_l[p][:], op=ALU.add),
                wait=[ev_s2, tern_evac_slot[sl]])
            rd_hq[p] = ev_o
            rd_l[p] = ev_o

            def pe_part():
                pbank = (psD if sl == 0 else psC).bitcast(BF16)
                for kc in range(8):
                    top(lambda e, kc=kc, pbank=pbank: e.transpose(
                        pbank[:, kc * 128:(kc + 1) * 128],
                        t_out[sl][:, kc * 128:(kc + 1) * 128], idb[:]),
                        wait=[ev_o, ev_consts, tern_evac_slot[sl]])
                evt = em.now("s_pe")
                tern_evac_slot[sl] = vop(lambda e, pbank=pbank: e.tensor_scalar(
                    out=dst_fn(),
                    in0=pbank[:, 0:1024].rearrange("p (kc c) -> p kc c", kc=8),
                    scalar1=0.5, scalar2=None, op0=ALU.mult), wait=[evt])
            return pe_part
        return emit

    def drip_flush(n=2):
        for _ in range(n):
            if drip_pending[0]:
                drip_pending[0]()
                drip_pending[0] = None
            if drip_q:
                drip_pending[0] = drip_q.pop(0)()

    def drip_drain():
        while drip_q or drip_pending[0]:
            drip_flush(1)

    ev_recip = None
    for nt in range(32):
        drip_q.append(scale_job(nt, 0, 1, 1024, p_scale[:, nt:nt + 1],
                                p_thr[:, nt:nt + 1], p_nthr[:, nt:nt + 1]))

    def recip_job():
        nonlocal ev_recip
        ev_recip = vop(lambda e: e.reciprocal(p_recip, p_scale))
    drip_q.append(recip_job)
    for nt in range(8):
        for cc in range(4):
            drip_q.append(scale_job(32 + nt * 4 + cc, cc, 4, 4096,
                                    p_scale2[:, nt:nt + 1], p_thr2[:, nt:nt + 1],
                                    p_nthr2[:, nt:nt + 1]))
    for nt in range(32):
        drip_q.append(cmp_job(64 + nt, p_thr[:, nt:nt + 1], p_nthr[:, nt:nt + 1],
                              (lambda nt=nt: fc8[:, :, nt * 128:(nt + 1) * 128])))
    for nt in range(8):
        for cc in range(4):
            drip_q.append(cmp_job(
                96 + nt * 4 + cc, p_thr2[:, nt:nt + 1], p_nthr2[:, nt:nt + 1],
                (lambda nt=nt, cc=cc: pj8[:, cc * 8:(cc + 1) * 8,
                                          nt * 128:(nt + 1) * 128])))

    ev_uts = None

    def uts_build():
        nonlocal ev_uts
        top(lambda e: e.transpose(psM[0:32, 0:128], p_recip, idf[:]),
            wait=[ev_recip, ev_consts, em.now("s_v")])
        vop(lambda e: e.tensor_copy(q_r32, psM[0:32, 0:128]), wait=[em.now("s_pe")])
        ev1 = dsy("qut", lambda e: e.dma_start(out=dsc[:], in_=q_r32),
                  wait=[em.now("s_v")])
        ev2 = dgp("qrp", lambda e: e.dma_start(
            out=q_rrow, in_=dsc.rearrange("a b -> (a b)").unsqueeze(0)), wait=[ev1])
        ev3 = dsy("qsm", lambda e: e.dma_start(out=q_ut, in_=ffnu[:]),
                  wait=[em.now("s_v"), em.now("s_pe")])
        for j in range(8):
            top(lambda e, j=j: e.matmul(psM[0:16, 0:512], onesb[0:1, 0:16],
                                        q_rrow[0:1, j * 512:(j + 1) * 512],
                                        start=True, stop=True),
                wait=[ev2, ev3, ev_ones] + ([ev_uts] if ev_uts else []))
            evb = em.now("s_pe")
            ev_uts = vop(lambda e, j=j: e.tensor_tensor(
                out=uts[:, j * 512:(j + 1) * 512],
                in0=q_ut[:, j * 512:(j + 1) * 512],
                in1=psM[0:16, 0:512], op=ALU.mult), wait=[evb])

    prev_p3_v = em.now("s_v")
    prev_p3_pe = em.now("s_pe")
    prev_H_pe = em.now("s_pe")
    prev_s2 = em.now("s_v")
    prev_f1rs = em.now("s_sc")
    prev_store = None
    prev_ew = None
    prev_ebc = None
    prev_erow_rd = None
    prev_w0 = None
    prev_w2 = None
    prev_w3 = None
    prev_tc = None
    prev_dl_rd = []
    prev_rr_rd = []
    prev_er_rd = []
    ev_prev_states = None
    ev_flagged = None

    wslot = [0]
    wslot_busy = [None] * 3

    def wtile_load(ct):
        s = wslot[0] % 3
        wslot[0] += 1
        w = [wslot_busy[s], ev_casts] if wslot_busy[s] is not None else [ev_casts]
        ev = dsy(f"wr{s}", lambda e, ct=ct, s=s: e.dma_start(out=wring[:, s, :, :],
                                                             in_=win_s[ct]), wait=w)
        return s, ev

    s_dt0, ev_wdt0 = wtile_load(33)
    ev_dtprev = None
    ev_dtst = None
    for b in range(NB):
        evx = dsy("qx", lambda e, b=b: e.dma_start(
            out=xob[:, :, :], in_=x_s[:, :, b * TB:(b + 1) * TB]),
            wait=[ev_casts, ev_dtprev])
        for kc in range(8):
            top(lambda e, kc=kc, s_dt0=s_dt0: e.matmul(
                psM[64:80, 0:TB], wring[:, s_dt0, kc, 0:16], xob[:, kc, :],
                start=(kc == 0), stop=(kc == 7)),
                wait=[ev_wdt0, evx, em.now("s_v")])
        ev_dtprev = em.now("s_pe")
        vop(lambda e: e.tensor_copy(p_t1, psM[64:80, 0:TB]),
            wait=[ev_dtprev, ev_dtst, ("qon", 16)])
        ev_dtst = dsy("qdt", lambda e, b=b: e.dma_start(
            out=dt_all[:, b * TB:(b + 1) * TB], in_=p_t1), wait=[em.now("s_v")])
    wslot_busy[s_dt0] = ev_dtprev
    ev_dt_all = em.now("qdt")


    prev_xob_rd = em.now("s_pe")
    pending_x = None
    prev_peb = None
    prev_dtwmm = [None]

    def p0_chain(blk):
        nonlocal prev_ew, prev_ebc, prev_erow_rd, prev_w0, prev_w2, prev_w3
        nonlocal prev_peb
        tok0 = blk * TB
        evr = dsy("qdt", lambda e, tok0=tok0: e.dma_start(
            out=p_E, in_=dt_all[:, tok0:tok0 + TB]),
            wait=[ev_dt_all, prev_peb])
        sop(lambda e: e.activation(out=p_t1, in_=p_E, func=AF.Exp, bias=p_dtb),
            wait=[evr, ev_small, prev_ew, ("qon", 16)])
        ev_dt = sop(lambda e: e.activation(out=p_dt, in_=p_t1, func=AF.Ln,
                                           bias=1.0), wait=[prev_dtwmm[0]])
        ev_da = vop(lambda e: e.tensor_scalar(out=p_t1, in0=p_dt, scalar1=p_acol,
                                              scalar2=None, op0=ALU.mult),
                    wait=[ev_dt, ev_acol])
        ev_cs = vop(lambda e: e.tensor_tensor_scan(p_cs, p_gate, p_t1, 0.0,
                                                   op0=ALU.mult, op1=ALU.add),
                    wait=[ev_da, ev_consts, prev_w2])
        ev_E = sop(lambda e: e.activation(out=p_E, in_=p_cs, func=AF.Exp),
                   wait=[ev_cs, prev_w3])
        ev_ec = sop(lambda e: e.activation(out=p_t1[:, 0:NC], in_=p_cs[:, CH - 1:TB:CH],
                                           func=AF.Exp), wait=[ev_cs, ev_da])
        ev_ew = dsy("qde", lambda e: e.dma_start(out=de[:], in_=p_t1[:, 0:NC]),
                    wait=[ev_ec, prev_erow_rd])
        ev_er = dsy("qerow", lambda e: e.dma_start(
            out=p_erow.rearrange("p (c h) -> p c h", c=NC),
            in_=de.rearrange("h c -> c h").unsqueeze(0),
            allow_slow_non_contiguous=True),
            wait=[ev_ew, prev_ebc])
        ev_ncs = vop(lambda e: e.tensor_scalar(out=p_ncs, in0=p_cs, scalar1=-1.0,
                                               scalar2=None, op0=ALU.mult),
                     wait=[ev_cs, prev_w0])
        for c in range(NC):
            vop(lambda e, c=c: e.tensor_scalar(
                out=p_wdec[:, c * CH:(c + 1) * CH], in0=p_cs[:, c * CH:(c + 1) * CH],
                scalar1=p_cs[:, c * CH + CH - 1:c * CH + CH], scalar2=None,
                op0=ALU.subtract), wait=[ev_cs, prev_dtwmm[0]])
        ev_subs = em.now("s_v")
        for c in range(NC):
            sop(lambda e, c=c: e.activation(out=p_wdec[:, c * CH:(c + 1) * CH],
                                            in_=p_wdec[:, c * CH:(c + 1) * CH],
                                            func=AF.Exp, scale=-1.0), wait=[ev_subs])
        ev_exps = em.now("s_sc")
        ev_wd = []
        for c in range(NC):
            ev_wd.append(vop(lambda e, c=c: e.tensor_tensor(
                out=p_wdec[:, c * CH:(c + 1) * CH], in0=p_wdec[:, c * CH:(c + 1) * CH],
                in1=p_dt[:, c * CH:(c + 1) * CH], op=ALU.mult), wait=[ev_exps]))
        w_rows = [ev_ncs] + prev_dl_rd
        w_rows2 = list(prev_dl_rd) + list(prev_rr_rd)
        w_rows3 = list(prev_er_rd)
        if prev_store:
            w_rows.append(prev_store)
        ev_w0 = dsy("qw0", lambda e: e.dma_start(out=drows[0], in_=p_ncs), wait=w_rows)
        ev_w2 = dsy("qw2", lambda e: e.dma_start(out=drows[2], in_=p_cs),
                    wait=[ev_cs] + w_rows2)
        ev_eb = vop(lambda e: e.tensor_copy(p_Eb, p_E), wait=[ev_E])
        ev_w3 = dsy("qw3", lambda e: e.dma_start(out=drows3b, in_=p_Eb),
                    wait=[ev_eb] + w_rows3)
        prev_ew = ev_ew
        prev_erow_rd = ev_er
        prev_peb = ev_eb
        prev_w0, prev_w2, prev_w3 = ev_w0, ev_w2, ev_w3
        return dict(ev_wd=ev_wd, ev_er=ev_er, ev_w0=ev_w0, ev_w2=ev_w2,
                    ev_w3=ev_w3)

    ch = p0_chain(0)
    ch_pending = None
    for blk in range(NB):
        own = blk >= 4
        tok0 = blk * TB
        if blk > 0 and ch_pending is not None:
            ch = ch_pending
            ch_pending = None
        elif blk > 0:
            prev_dl_rd = [em.now("qdl0"), em.now("qdl1")]
            prev_rr_rd = [em.now("qrr0"), em.now("qrr1")]
            prev_er_rd = [em.now("qer0"), em.now("qer1")]
            ch = p0_chain(blk)
        if pending_x is None:
            ev_x = dsy("qx", lambda e, tok0=tok0: e.dma_start(
                out=xob[:, :, :], in_=x_s[:, :, tok0:tok0 + TB]),
                wait=[prev_xob_rd, ev_casts])
        else:
            ev_x = pending_x

        ev_w0, ev_w2, ev_w3 = ch["ev_w0"], ch["ev_w2"], ch["ev_w3"]
        for c in range(NC):
            top(lambda e, c=c: e.matmul(psM[0:64, 128 + c * 16:128 + (c + 1) * 16],
                                        onesf[64:65, 0:64],
                                        p_erow[0:1, c * 16:(c + 1) * 16],
                                        start=True, stop=True),
                wait=[ch["ev_er"], prev_s2])
        ev_ebc = em.now("s_pe")
        prev_ebc = ev_ebc
        ev_ecols = vop(lambda e: e.tensor_copy(
            p_ecols, psM[0:64, 128:128 + NC * 16].rearrange("p (c k) -> p c k", c=NC)),
            wait=[ev_ebc])
        for c in range(NC):
            top(lambda e, c=c: e.transpose(psM[0:128, 256 + c * 16:256 + (c + 1) * 16],
                                           p_dt[:, c * CH:(c + 1) * CH],
                                           idf[64:80, 64:80]),
                wait=[ch["ev_wd"][c], ev_consts, ev_ecols])
            top(lambda e, c=c: e.transpose(psM[0:128, 320 + c * 16:320 + (c + 1) * 16],
                                           p_wdec[:, c * CH:(c + 1) * CH],
                                           idf[64:80, 64:80]),
                wait=[ch["ev_wd"][c], ev_consts, ev_ecols])
        ev_dtwmm = em.now("s_pe")
        prev_dtwmm[0] = ev_dtwmm
        vop(lambda e: e.tensor_copy(
            p_dtwT[:, :, 0:16],
            psM[0:128, 256:256 + NC * 16].rearrange("p (c k) -> p c k", c=NC)),
            wait=[ev_dtwmm])
        ev_dtwT = vop(lambda e: e.tensor_copy(
            p_dtwT[:, :, 16:32],
            psM[0:128, 320:320 + NC * 16].rearrange("p (c k) -> p c k", c=NC)))

        s_bc, ev_wbc = wtile_load(32)
        for kc in range(8):
            top(lambda e, s_bc=s_bc, kc=kc: e.matmul(
                psA[:, 0:TB], wring[:, s_bc, kc, :], xob[:, kc, :],
                start=(kc == 0), stop=(kc == 7)),
                wait=[ev_wbc, ev_x, prev_f1rs])
        ev_bcmm = em.now("s_pe")
        wslot_busy[s_bc] = ev_bcmm
        xbcb = a_xbc[0]
        w_ev = [ev_bcmm, prev_p3_v]
        if prev_store:
            w_ev.append(prev_store)
        vop(lambda e: e.tensor_copy(xbcb[:, 0:3], halo[:, 16, 0:3]), wait=w_ev)
        vop(lambda e: e.tensor_copy(xbcb[:, 3:3 + TB], psA[:, 0:TB]))
        vop(lambda e: e.tensor_copy(halo[:, 16, 0:3], psA[:, TB - 3:TB]))
        ev_xbc_bc = em.now("s_v")
        for tap in range(DCONV):
            top(lambda e, tap=tap: e.matmul(
                psB[:, 0:TB], cdiag[:, 16, tap, :], xbcb[:, tap:tap + TB],
                start=(tap == 0), stop=(tap == 3)), wait=[ev_xbc_bc, ev_cdiag])
        ev_convbc = em.now("s_pe")
        ev_bcs = sop(lambda e: e.activation(out=bcs[:], in_=psB[:, 0:TB], func=AF.Silu,
                                            bias=p_convb[:, 16:17]),
                     wait=[ev_convbc, ev_small, prev_tc])
        ev_tc = dsy("qtc", lambda e: e.dma_start(out=tcc[:], in_=bcs[64:128, :]),
                    wait=[ev_bcs, prev_p3_v, prev_p3_pe])
        prev_tc = ev_tc
        for c in range(NC):
            top(lambda e, c=c: e.transpose(psT[:, 512 + c * 64:512 + (c + 1) * 64],
                                           bcs[0:64, c * CH:(c + 1) * CH],
                                           idb[0:64, 0:64]),
                wait=[ev_bcs, ev_convbc])
        ev_btmm = em.now("s_pe")
        ev_bt = vop(lambda e: e.tensor_copy(
            p_bt, psT[:, 512:512 + NC * 64].rearrange("p (c k) -> p c k", c=NC)),
            wait=[ev_btmm])
        ev_cbm = None
        if own:
            tg = [e for e in tern_evac_slot if e] if blk == 4 else []
            for c in range(NC):
                top(lambda e, c=c: e.matmul(
                    psD[:, c * 128:(c + 1) * 128], bcs[0:64, c * CH:(c + 1) * CH],
                    tcc[:, c * CH:(c + 1) * CH], start=True, stop=True),
                    wait=[ev_tc, ev_bcs] + tg)
            ev_cbmm = em.now("s_pe")
            ev_cbm = vop(lambda e: e.tensor_tensor(
                out=p_cbm, in0=psD[:, 0:512].rearrange("p (c k) -> p c k", c=NC),
                in1=maskT[:].unsqueeze(1).broadcast_to((128, NC, 128)), op=ALU.mult),
                wait=[ev_cbmm, ev_consts])

        if own:
            ev_dl_used = [None, None]
            last_min = None
            ev_exp_par = [None, None]
            last_S = [None, None]
            for h in range(NH):
                pp = h % 2
                wrow = [ev_w0, ev_w2, ("qon", 16)]
                if ev_dl_used[pp] is not None:
                    wrow.append(ev_dl_used[pp])
                evd1 = dsy(f"qdl{pp}", lambda e, h=h, pp=pp: e.dma_start(
                    out=p_dl[pp], in_=drows[0:2, h, :]), wait=wrow)
                evd2 = dsy(f"qrr{pp}", lambda e, h=h, pp=pp: e.dma_start(
                    out=p_rr[pp], in_=drows[1:3, h, :]),
                    wait=[ev_w2, ("qon", 16)] + ([ev_dl_used[pp]] if ev_dl_used[pp] else []))
                for c in range(NC):
                    top(lambda e, pp=pp, c=c: e.matmul(
                        psD[:, c * 128:(c + 1) * 128],
                        p_dl[pp][:, c * CH:(c + 1) * CH],
                        p_rr[pp][:, c * CH:(c + 1) * CH], start=True, stop=True),
                        wait=[evd1, evd2, ev_cbm if h == 0 else last_min])
                ev_dmm = em.now("s_pe")
                ev_dl_used[pp] = ev_dmm
                last_min = vop(lambda e: e.tensor_scalar(
                    out=a_dm, in0=psD[:, 0:512], scalar1=0.0, scalar2=None,
                    op0=ALU.min), wait=[ev_dmm, ev_exp_par[pp] if h >= 2 else None])
                w_exp = [last_min]
                if last_S[pp] is not None:
                    w_exp.append(last_S[pp])
                ev_exp = sop(lambda e, pp=pp: e.activation(
                    out=a_sexp[pp][:, :, :], in_=a_dm.rearrange("p (c k) -> p c k", c=NC),
                    func=AF.Exp), wait=w_exp)
                ev_exp_par[pp] = ev_exp
                for c in range(NC):
                    vop(lambda e, h=h, pp=pp, c=c: e.scalar_tensor_tensor(
                        out=a_m[pp][:, c, :], in0=p_cbm[:, c, :],
                        scalar=p_dtwT[:, c, h:h + 1], in1=disk[:, h, :],
                        op0=ALU.mult, op1=ALU.add),
                        wait=[ev_cbm, ev_dtwT, ev_disk])
                last_S[pp] = vop(lambda e, h=h, pp=pp: e.tensor_tensor(
                    out=a_scores[:, h, :, :], in0=a_sexp[pp][:, :, :],
                    in1=a_m[pp][:, :, :], op=ALU.mult), wait=[ev_exp])
            ev_scores = em.now("s_v")

        st = [dict() for _ in range(NH)]
        xob_rd = [em.now("s_pe")]
        ev_conv_done = None
        ev_xT_ev = None
        ev_cexp_ev = None
        ev_er_used = [None, None]
        last_y = None
        last_sil = None
        last_zsil = None
        last_xbc_evac = None

        def stage_a1(h):
            nonlocal last_xbc_evac, ev_conv_done
            d = st[h]
            d["pp"] = h % 2
            s_x, ev_wx = wtile_load(16 + h)
            if own:
                d["s_z"], d["ev_wz"] = wtile_load(h)
                wE = [ev_w3]
                if ev_er_used[h % 2] is not None:
                    wE.append(ev_er_used[h % 2])
                d["ev_erh"] = dsy(f"qer{h % 2}", lambda e, h=h: e.dma_start(
                    out=p_er[h % 2], in_=drows3b[h, :].unsqueeze(0)), wait=wE)
            w_inp = [ev_wx, ev_x]
            if h == 0:
                w_inp += [ev_xbc_bc, ev_bcmm]
            else:
                w_inp += [last_zsil if own else last_xbc_evac]
            for kc in range(8):
                top(lambda e, s_x=s_x, kc=kc: e.matmul(
                    psA[:, 0:TB], wring[:, s_x, kc, :], xob[:, kc, :],
                    start=(kc == 0), stop=(kc == 7)), wait=w_inp)
            ev_xmm = em.now("s_pe")
            wslot_busy[s_x] = ev_xmm
            xob_rd[0] = ev_xmm
            w_ev = [ev_xmm]
            if ev_conv_done is not None:
                w_ev.append(ev_conv_done)
            xbch = a_xbc[1]
            vop(lambda e, h=h: e.tensor_copy(xbch[:, 0:3], halo[:, h, 0:3]), wait=w_ev)
            vop(lambda e: e.tensor_copy(xbch[:, 3:3 + TB], psA[:, 0:TB]))
            vop(lambda e, h=h: e.tensor_copy(halo[:, h, 0:3], psA[:, TB - 3:TB]))
            d["ev_xbc"] = em.now("s_v")
            last_xbc_evac = d["ev_xbc"]

        def stage_a2(h):
            nonlocal ev_conv_done, last_sil, last_zsil
            d = st[h]
            pp = d["pp"]
            xbch = a_xbc[1]
            w_cv = [d["ev_xbc"], last_sil if h > 0 else ev_bcs]
            for tap in range(DCONV):
                top(lambda e, h=h, tap=tap: e.matmul(
                    psB[:, 0:TB], cdiag[:, h, tap, :], xbch[:, tap:tap + TB],
                    start=(tap == 0), stop=(tap == 3)), wait=w_cv)
            ev_conv_done = em.now("s_pe")
            d["ev_sil"] = sop(lambda e, h=h, pp=pp: e.activation(
                out=a_xssm[pp][:], in_=psB[:, 0:TB], func=AF.Silu,
                bias=p_convb[:, h:h + 1]), wait=[ev_conv_done, ev_xT_ev])
            last_sil = d["ev_sil"]
            if own:
                for kc in range(8):
                    top(lambda e, d=d, kc=kc: e.matmul(
                        psA[:, 0:TB], wring[:, d["s_z"], kc, :], xob[:, kc, :],
                        start=(kc == 0), stop=(kc == 7)),
                        wait=[d["ev_wz"], d["ev_xbc"]])
                ev_zmm = em.now("s_pe")
                wslot_busy[d["s_z"]] = ev_zmm
                xob_rd[0] = ev_zmm
                d["ev_zsil"] = sop(lambda e, pp=pp: e.activation(
                    out=a_zsil[pp][:], in_=psA[:, 0:TB], func=AF.Silu),
                    wait=[ev_zmm, last_y if h >= 2 else None])
                last_zsil = d["ev_zsil"]

        def stage_b(h):
            nonlocal ev_xT_ev, ev_cexp_ev, last_y, ev_prev_states
            d = st[h]
            pp = d["pp"]
            w_xt = [d["ev_sil"], ev_xT_ev if h > 0 else ev_bt]
            for c in range(NC):
                top(lambda e, pp=pp, c=c: e.transpose(
                    psT[:, c * 128:(c + 1) * 128],
                    a_xssm[pp][:, c * CH:(c + 1) * CH], idb[:]), wait=w_xt)
            ev_xTmm = em.now("s_pe")
            ev_xT_ev = vop(lambda e, pp=pp: e.tensor_copy(
                a_xT[pp][:, :, :], psT[:, 0:512].rearrange("p (c k) -> p c k", c=NC)),
                wait=[ev_xTmm])
            ev_bd = vop(lambda e, h=h, pp=pp: e.tensor_tensor(
                out=a_bdec[pp][:, :, :], in0=p_bt[:, :, :],
                in1=p_dtwT[:, :, 16 + h:17 + h].broadcast_to((128, NC, 64)),
                op=ALU.mult), wait=[ev_bt, ev_dtwT, em.now("s_pe")])
            if own:
                w_cb = [d["ev_erh"], ev_cexp_ev if ev_cexp_ev else ev_cbmm]
                if blk == 4 and h == 0:
                    w_cb += [e for e in tern_evac_slot if e]
                for c in range(NC):
                    top(lambda e, pp=pp, c=c: e.matmul(
                        psC[0:64, c * 128:(c + 1) * 128], onesb[64:65, 0:64],
                        p_er[pp][0:1, c * CH:(c + 1) * CH], start=True, stop=True),
                        wait=w_cb)
                ev_cbc = em.now("s_pe")
                ev_er_used[pp] = ev_cbc
                w_cx = [ev_cbc, ev_tc]
                if h == 0 and prev_store:
                    w_cx.append(prev_store)
                ev_cexp_ev = vop(lambda e, pp=pp: e.tensor_tensor(
                    out=a_cexp[pp][:, :, :],
                    in0=tcc[:].rearrange("p (c k) -> p c k", c=NC),
                    in1=psC[0:64, 0:512].rearrange("p (c k) -> p c k", c=NC),
                    op=ALU.mult), wait=w_cx)
            for c in range(NC):
                wst = [ev_xT_ev, ev_bd]
                if ev_prev_states is not None:
                    wst.append(ev_prev_states)
                top(lambda e, pp=pp, c=c: e.matmul(
                    psS[0:64, c * 128:(c + 1) * 128], a_bdec[pp][:, c, :],
                    a_xT[pp][:, c, :], start=True, stop=True), wait=wst)
                if own:
                    w_yd = [ev_scores, ev_xT_ev]
                    if last_y is not None:
                        w_yd.append(last_y)
                    top(lambda e, h=h, pp=pp, c=c: e.matmul(
                        psY[:, c * 128:(c + 1) * 128], a_xT[pp][:, c, :],
                        a_scores[:, h, c, :], start=True, stop=False), wait=w_yd)
                    w_yo = [ev_cexp_ev, ev_r0]
                    if ev_flagged is not None and blk == 4:
                        w_yo.append(ev_flagged)
                    top(lambda e, h=h, pp=pp, c=c: e.matmul(
                        psY[:, c * 128:(c + 1) * 128], rst[:, h * 128:(h + 1) * 128],
                        a_cexp[pp][:, c, :], start=False, stop=True), wait=w_yo)
                ev_y_c = em.now("s_pe")
                ev_prev_states = vop(lambda e, h=h, c=c: e.scalar_tensor_tensor(
                    out=rst[:, h * 128:(h + 1) * 128],
                    in0=rst[:, h * 128:(h + 1) * 128],
                    scalar=p_ecols[:, c, h:h + 1],
                    in1=psS[0:64, c * 128:(c + 1) * 128],
                    op0=ALU.mult, op1=ALU.add),
                    wait=[ev_y_c, ev_ecols, ev_r0])
            if own:
                last_y = vop(lambda e, h=h, pp=pp: e.tensor_tensor(
                    out=yg[:, h, :], in0=psY[:, 0:512], in1=a_zsil[pp][:],
                    op=ALU.mult), wait=[em.now("s_pe"), d["ev_zsil"]])

        stage_a1(0)
        stage_a2(0)
        if not own:
            drip_flush(1)
        for h in range(1, NH):
            stage_a1(h)
            stage_b(h - 1)
            if not own:
                drip_flush(1)
            stage_a2(h)
            if not own:
                drip_flush(1)
        stage_b(NH - 1)
        if blk == 3:
            drip_drain()
            uts_build()
            prev_s2 = ev_uts
        if blk + 1 < NB:
            pending_x = dsy("qx", lambda e, tok0=tok0 + TB: e.dma_start(
                out=xob[:, :, :], in_=x_s[:, :, tok0:tok0 + TB]),
                wait=[xob_rd[0], ev_casts])
        ev_H_v = em.now("s_v")
        ev_H_pe = em.now("s_pe")
        prev_H_pe = ev_H_pe
        prev_xob_rd = xob_rd[0]

        if blk == 3:
            rtmp = g_s.tensor.ap().bitcast(F32)[0:64, 0:2048]
            vop(lambda e: e.tensor_scalar(out=rtmp, in0=rst[:], scalar1=p_flagc,
                                          scalar2=None, op0=ALU.mult),
                wait=[ev_H_v, ev_H_pe])
            ev_flagged = vop(lambda e: e.tensor_copy(rst[:], rtmp))

        if dbg and blk == 4:
            dsy("qdbg", lambda e: e.dma_start(out=dbg_t["d_dt"][:], in_=p_dt), wait=[ev_H_v])
            dsy("qdbg", lambda e: e.dma_start(out=dbg_t["d_cs"][:], in_=p_cs), wait=[ev_H_v])
            vop(lambda e: e.tensor_copy(g_tmp[:, 0:TB], a_xssm[(NH - 1) % 2][:]),
                wait=[ev_H_v, ev_H_pe])
            dsy("qdbg", lambda e: e.dma_start(out=dbg_t["d_xssm"][:], in_=g_tmp[:, 0:TB]),
                wait=[em.now("s_v")])
            vop(lambda e: e.tensor_copy(g_tmp[:, 0:TB], a_scores[:, 0, :, :]),
                wait=[em.now("qdbg")])
            dsy("qdbg", lambda e: e.dma_start(out=dbg_t["d_scores"][:], in_=g_tmp[:, 0:TB]),
                wait=[em.now("s_v")])
            rtmp2 = g_s.tensor.ap().bitcast(F32)[0:64, 0:2048]
            vop(lambda e: e.tensor_copy(rtmp2, rst[:]), wait=[em.now("qdbg")])
            dsy("qdbg", lambda e: e.dma_start(out=dbg_t["d_r"][:], in_=rtmp2),
                wait=[em.now("s_v")])
            vop(lambda e: e.tensor_copy(g_tmp[:, 0:512].bitcast(BF16)[:, 0:1024],
                                        fc8[:, 0, 0:1024]), wait=[em.now("qdbg")])
            dgp("qdbg2", lambda e: e.dma_start(out=dbg_t["d_fc8"][:],
                                      in_=g_tmp[:, 0:512].bitcast(BF16)[:, 0:1024]),
                wait=[em.now("s_v")])
            vop(lambda e: e.tensor_copy(g_tmp[:, 0:32], p_scale2[:, 0:8]), wait=[em.now("qdbg2")]) if False else None
            vop(lambda e: e.tensor_copy(g_tmp[:, 0:8], p_scale2), wait=[em.now("qdbg2")])
            dsy("qdbg", lambda e: e.dma_start(out=dbg_t["d_scale"][:, 0:8], in_=g_tmp[:, 0:8]),
                wait=[em.now("s_v")])
            for pk in range(16):
                vop(lambda e, pk=pk: e.tensor_copy(
                    g_tmp[:, 0:512].bitcast(BF16)[:, 0:1024] if False else g_tmp[:].bitcast(BF16)[:, 0:1024],
                    pj8.tensor.ap().bitcast(FP8)[:, pk * 2048:(pk + 1) * 2048][:, 0:1024] if False else
                    pj8[:, pk * 2, :].rearrange("p c -> p c") if False else
                    pj8.tensor.ap()[:, pk * 2048:pk * 2048 + 1024]),
                    wait=[em.now("qdbg2")])
                dgp("qdbg2", lambda e, pk=pk: e.dma_start(
                    out=dbg_t["d_pj8"][:, pk * 2048:pk * 2048 + 1024],
                    in_=g_tmp[:].bitcast(BF16)[:, 0:1024]), wait=[em.now("s_v")])
                vop(lambda e, pk=pk: e.tensor_copy(
                    g_tmp[:].bitcast(BF16)[:, 0:1024],
                    pj8.tensor.ap()[:, pk * 2048 + 1024:pk * 2048 + 2048]),
                    wait=[em.now("qdbg2")])
                dgp("qdbg2", lambda e, pk=pk: e.dma_start(
                    out=dbg_t["d_pj8"][:, pk * 2048 + 1024:pk * 2048 + 2048],
                    in_=g_tmp[:].bitcast(BF16)[:, 0:1024]), wait=[em.now("s_v")])

        if not own:
            prev_p3_v = em.now("s_v")
            prev_p3_pe = ev_H_pe
            continue

        ev_xf = dsy("qxf", lambda e, tok0=tok0: e.dma_start(
            out=g_xf[:, :, :],
            in_=xfm.rearrange("(a p) t -> p a t", p=128)[:, :, tok0:tok0 + TB]),
            wait=[ev_H_v, ev_H_pe, prev_store])
        ev_wo_used = [None, None]
        ev_s_tiles = []
        for dt_ in range(8):
            sl = dt_ % 2
            w = [ev_H_pe]
            if ev_wo_used[sl] is not None:
                w.append(ev_wo_used[sl])
            ev_wo = dsy(f"wo{sl}", lambda e, dt_=dt_, sl=sl: e.dma_start(
                out=woutr[:, sl, :, :], in_=wout_s[dt_]), wait=w)
            pps = psA if sl == 0 else psB
            w_om = [ev_wo, ev_H_v]
            if dt_ >= 2:
                w_om.append(ev_s_tiles[dt_ - 2])
            for kc in range(16):
                top(lambda e, sl=sl, kc=kc, pps=pps: e.matmul(
                    pps[:, 0:TB], woutr[:, sl, kc, :], yg[:, kc, :],
                    start=(kc == 0), stop=(kc == 15)), wait=w_om)
            ev_omm = em.now("s_pe")
            ev_wo_used[sl] = ev_omm
            ev_s_tiles.append(vop(lambda e, dt_=dt_, pps=pps: e.tensor_tensor(
                out=g_s[:, dt_, :], in0=pps[:, 0:TB], in1=g_xf[:, dt_, :],
                op=ALU.add), wait=[ev_omm, ev_xf]))

        def layer_norm(src_ap, out_ap, ev_tiles):
            for i in range(8):
                top(lambda e, i=i: e.matmul(psD[0:1, 0:TB], onesf[:, 0:1],
                                            src_ap[:, i, :],
                                            start=(i == 0), stop=(i == 7)),
                    wait=(list(ev_tiles) if i == 0 else [ev_tiles[i]]) + [ev_consts])
            ev_mean = em.now("s_pe")
            ev_sq = None
            for i in range(8):
                w_sq = [ev_tiles[i]]
                if ev_sq is not None:
                    w_sq.append(ev_sq)
                evq = sop(lambda e, i=i: e.activation(out=g_sq[:], in_=src_ap[:, i, :],
                                                      func=AF.Square), wait=w_sq)
                ev_sq = top(lambda e, i=i: e.matmul(psC[0:1, 0:TB], onesb[:, 0:1],
                                                    g_sq[:], start=(i == 0),
                                                    stop=(i == 7)),
                            wait=[evq] + (list(ev_tiles) if i == 0 else []))
            ev_sqm = em.now("s_pe")
            vop(lambda e: e.tensor_scalar(out=p_mrow, in0=psD[0:1, 0:TB],
                                          scalar1=1.0 / 1024.0, scalar2=None,
                                          op0=ALU.mult), wait=[ev_mean, ev_sqm])
            vop(lambda e: e.tensor_scalar(out=p_var, in0=psC[0:1, 0:TB],
                                          scalar1=1.0 / 1024.0, scalar2=None,
                                          op0=ALU.mult))
            vop(lambda e: e.tensor_tensor(out=p_vt1, in0=p_mrow, in1=p_mrow,
                                          op=ALU.mult))
            vop(lambda e: e.tensor_tensor(out=p_var, in0=p_var, in1=p_vt1,
                                          op=ALU.subtract))
            ev_v = vop(lambda e: e.tensor_scalar(out=p_var, in0=p_var, scalar1=EPS,
                                                 scalar2=None, op0=ALU.add))
            sop(lambda e: e.activation(out=p_vt1, in_=p_var, func=AF.Ln),
                wait=[ev_v])
            ev_rstd = sop(lambda e: e.activation(out=p_rstdr, in_=p_vt1, func=AF.Exp,
                                                 scale=-0.5))
            top(lambda e: e.matmul(psD[:, 0:TB], onesf[0:1, :], p_mrow[0:1, :],
                                   start=True, stop=True), wait=[em.now("s_v")])
            top(lambda e: e.matmul(psC[:, 0:TB], onesf[0:1, :], p_rstdr[0:1, :],
                                   start=True, stop=True), wait=[ev_rstd])
            ev_bcast = em.now("s_pe")
            evs = []
            for i in range(8):
                vop(lambda e, i=i: e.tensor_tensor(out=g_tmp[:], in0=src_ap[:, i, :],
                                                   in1=psD[:, 0:TB], op=ALU.subtract),
                    wait=[ev_bcast])
                evs.append(vop(lambda e, i=i: e.tensor_tensor(
                    out=out_ap[:, i, :], in0=g_tmp[:], in1=psC[:, 0:TB],
                    op=ALU.mult)))
            return evs

        ev_x2_tiles = layer_norm(g_s, g_x2b, ev_s_tiles)
        ev_x2 = em.now("s_v")
        if blk + 1 < NB:
            prev_dl_rd = [em.now("qdl0"), em.now("qdl1")]
            prev_rr_rd = [em.now("qrr0"), em.now("qrr1")]
            prev_er_rd = [em.now("qer0"), em.now("qer1")]
            ch_pending = p0_chain(blk + 1)

        if dbg and blk == 4:
            for i in range(8):
                vop(lambda e, i=i: e.tensor_copy(g_tmp[:], g_s[:, i, :]),
                    wait=[ev_x2, em.now("qdbg")])
                dsy("qdbg", lambda e, i=i: e.dma_start(out=dbg_t["d_s"][:, i * TB:(i + 1) * TB],
                                               in_=g_tmp[:]), wait=[em.now("s_v")])
            for i in range(8):
                vop(lambda e, i=i: e.tensor_copy(g_tmp[:], g_x2b[:, i, :]),
                    wait=[ev_x2, em.now("qdbg")])
                dsy("qdbg", lambda e, i=i: e.dma_start(out=dbg_t["d_x2"][:, i * TB:(i + 1) * TB],
                                               in_=g_tmp[:]), wait=[em.now("s_v")])
            dgp("qdbg2", lambda e: e.dma_start(out=dbg_t["d_yg"][:], in_=yg_f), wait=[ev_x2])
            ev_dbgdone = em.now("qdbg")

        for kc in range(8):
            top(lambda e, kc=kc: e.matmul(psM[0:16, 0:TB], p_vb[:, kc, :],
                                          g_x2b[:, kc, :], start=(kc == 0),
                                          stop=(kc == 7)),
                wait=[ev_x2_tiles[kc], ev_smgp, ev_ecols, ev_dtwT])
        ev_s1m = em.now("s_pe")
        ev_s1 = vop(lambda e: e.tensor_copy(g_s1[:], psM[0:16, 0:TB]), wait=[ev_s1m])
        top(lambda e: e.matmul(psM[0:16, 0:TB], p_fbt[:], g_s1[:], start=True,
                               stop=True), wait=[ev_s1])
        ev_s2m = em.now("s_pe")
        ev_s2 = vop(lambda e: e.tensor_scalar(out=g_s2[:], in0=psM[0:16, 0:TB],
                                              scalar1=p_gatec, scalar2=None,
                                              op0=ALU.mult), wait=[ev_s2m])
        prev_s2 = ev_s2

        def r2slot(hi):
            if hi < 16:
                return yg_f.bitcast(BF16)[:, hi * TB:(hi + 1) * TB]
            return g_r2a_hi[:, hi - 16, :]
        ev_s2_tiles = [None] * 8
        ev_o_done = [None, None]
        ev_r2_list = []
        for hi in range(32):
            pfc = psA if hi % 2 == 0 else psB
            wfc_w = [ev_x2, ev_uts, ev_s2]
            if len(ev_r2_list) >= 2:
                wfc_w.append(ev_r2_list[-2])
            for kc in range(8):
                top(lambda e, hi=hi, kc=kc, pfc=pfc: e.matmul(
                    pfc[:, 0:TB], fc8[:, kc, hi * 128:(hi + 1) * 128],
                    g_x2b[:, kc, :], start=(kc == 0), stop=False), wait=wfc_w)
            top(lambda e, hi=hi, pfc=pfc: e.matmul(
                pfc[:, 0:TB], uts[:, hi * 128:(hi + 1) * 128],
                g_s2[:], start=False, stop=True), wait=wfc_w)
            ev_hmm = em.now("s_pe")
            ev_rs = sop(lambda e, hi=hi, pfc=pfc: e.activation(
                out=g_rs[hi % 2][:], in_=pfc[:, 0:TB], func=AF.Relu,
                scale=p_scale[:, hi:hi + 1]),
                wait=[ev_hmm] + ([ev_xf] if hi < 2 else []))
            ev_r2_list.append(sop(lambda e, hi=hi: e.activation(
                out=r2slot(hi), in_=g_rs[hi % 2][:], func=AF.Square),
                wait=[ev_rs]))
        last_r2 = ev_r2_list[-1]
        prev_f1rs = em.now("s_sc")
        for dt_ in range(8):
            bank = psD if dt_ % 2 == 0 else psC
            w_pj = [last_r2]
            if ev_o_done[dt_ % 2] is not None:
                w_pj.append(ev_o_done[dt_ % 2])
            for hi in range(32):
                top(lambda e, hi=hi, dt_=dt_, bank=bank: e.matmul(
                    bank[:, 0:TB],
                    pj8[:, hi, dt_ * 128:(dt_ + 1) * 128], r2slot(hi),
                    start=(hi == 0), stop=(hi == 31)), wait=w_pj)
            ev_pjmm = em.now("s_pe")
            ev_o = vop(lambda e, dt_=dt_, bank=bank: e.scalar_tensor_tensor(
                out=g_s[:, dt_, :], in0=bank[:, 0:TB],
                scalar=p_scale2[:, dt_:dt_ + 1], in1=g_x2b[:, dt_, :],
                op0=ALU.mult, op1=ALU.add), wait=[ev_pjmm])
            ev_o_done[dt_ % 2] = ev_o
            ev_s2_tiles[dt_] = ev_o

        ev_out_tiles = layer_norm(g_s, g_xf, ev_s2_tiles)
        prev_store = dgp("qst", lambda e, blk=blk: e.dma_start(
            out=yfm.rearrange("(a p) t -> p a t", p=128)[:, :, (blk - 4) * TB:(blk - 3) * TB],
            in_=g_xf[:, :, :]), wait=ev_out_tiles)
        prev_p3_v = em.now("s_v")
        prev_p3_pe = em.now("s_pe")

    all_dma = ["qsm", "qsg", "qwt0", "qwt1", "qwt2", "qwt3", "qsc", "qrp",
               "qut", "wr0", "wr1", "wr2",
               "wo0", "wo1", "qx", "qxf", "qdl0", "qdl1", "qrr0", "qrr1",
               "qer0", "qer1", "qw0", "qw2", "qw3", "qde", "qerow", "qtc", "qdt",
               "qst", "qdbg", "qdbg2", "qon"]
    em.op("sync", lambda e: e.nop(), wait=[em.now(q) for q in all_dma])

    with nc.Block() as block:
        em.emit(block)
    return nc, dbg_t


_prog_cache = {}


def _get_prog(dbg=False):
    if dbg not in _prog_cache:
        _prog_cache[dbg] = build_program(dbg=dbg)
    return _prog_cache[dbg]


def _prep_weights(i):
    W_in = np.asarray(i["L0_attn_W_in"], np.float32)
    W_in_pad = np.concatenate([W_in, np.zeros((4352 - 4240, 1024), np.float32)], 0)
    win = np.ascontiguousarray(W_in_pad.reshape(34, 128, 8, 128).transpose(0, 3, 2, 1))
    W_out = np.asarray(i["L0_attn_W_out"], np.float32)
    wout = np.ascontiguousarray(W_out.reshape(8, 128, 16, 128).transpose(0, 3, 2, 1))
    conv_w = np.asarray(i["L0_attn_conv_w"], np.float32).reshape(NTIL_X, 128, DCONV)
    conv_b = np.ascontiguousarray(np.asarray(i["L0_attn_conv_b"], np.float32).reshape(NTIL_X, 128).T)
    return {
        "win_t": win,
        "wout_t": wout,
        "wfc": np.ascontiguousarray(np.asarray(i["L0_ffn_w_fc"], np.float32)),
        "wpj": np.ascontiguousarray(np.asarray(i["L0_ffn_w_proj"], np.float32)),
        "cvw": np.ascontiguousarray(conv_w),
        "cvb": np.ascontiguousarray(conv_b),
        "alog": np.ascontiguousarray(np.asarray(i["L0_attn_A_log"], np.float32).reshape(16, 1)),
        "dtbias": np.ascontiguousarray(np.asarray(i["L0_attn_dt_bias"], np.float32).reshape(16, 1)),
        "dskip": np.ascontiguousarray(np.asarray(i["L0_attn_D_skip"], np.float32).reshape(1, 16)),
        "ffnu": np.ascontiguousarray(np.asarray(i["L0_ffn_u"], np.float32).T),
        "ffnv": np.ascontiguousarray(
            np.asarray(i["L0_ffn_v"], np.float32).T.reshape(8, 128, 16).transpose(1, 0, 2)),
        "fbt": np.ascontiguousarray(np.asarray(i["L0_ffn_fast_b"], np.float32).T),
        "gatei": np.asarray(i["L0_ffn_gate"], np.float32).reshape(1, 1),
    }


def kernel(**inputs):
    nc, _ = _get_prog(dbg=False)
    wts = _prep_weights(inputs)
    x = np.asarray(inputs["L0_attn_x"], np.float32)
    in_maps = []
    for core in range(8):
        b, hf = core // 2, core % 2
        own = x[b, hf * T_OWN:(hf + 1) * T_OWN]
        ctx = x[b, 0:T_OWN] if hf == 1 else np.zeros((T_OWN, D_MODEL), np.float32)
        m = dict(wts)
        m["xfm"] = np.ascontiguousarray(np.concatenate([ctx, own], 0).T)
        m["flagi"] = np.asarray([[1.0 if hf == 1 else 0.0]], np.float32)
        in_maps.append(m)
    res = run_bass_kernel_spmd(nc, in_maps, core_ids=list(range(8)))
    out = np.zeros((4, 4096, 1024), np.float32)
    for core in range(8):
        b, hf = core // 2, core % 2
        out[b, hf * T_OWN:(hf + 1) * T_OWN] = res.results[core]["yfm"].T
    return out
